# version 16
# baseline (speedup 1.0000x reference)
import numpy as np
import ml_dtypes
import contextlib
import concourse.bass as bass
import concourse.mybir as mybir
from concourse.bass_utils import run_bass_kernel_spmd

F32 = mybir.dt.float32
BF16 = mybir.dt.bfloat16
AF = mybir.ActivationFunctionType
ALU = mybir.AluOpType
AX = mybir.AxisListType

HOIST = True
D, T, H, DH, HID = 1024, 2048, 16, 64, 4096
NSEQ = 2
NCORES = 8
EXP_SHIFT = -10.0

_bf = lambda a: np.ascontiguousarray(a).astype(ml_dtypes.bfloat16)
_f32 = lambda a: np.ascontiguousarray(a, dtype=np.float32)


def _swap_bits(i):
    return ((i & 0x1F) << 7) | (i & 0x60) | ((i >> 7) & 0x1F)

_PI = np.array([_swap_bits(i) for i in range(HID)])


def _stage_block(w_stage, s, chunk, perm, transpose):
    sig = 1 << s
    M = np.zeros((128, 128), np.float32)
    base = 128 * chunk
    p = np.arange(128)
    gi = _PI[base + p] if perm else base + p
    pl = p ^ (1 << (s - 7)) if perm else p ^ sig
    k = ((gi & ~(2 * sig - 1)) >> 1) | (gi & (sig - 1))
    lo = (gi & sig) == 0
    M[p[lo], p[lo]] = w_stage[k[lo], 0, 0]
    M[p[lo], pl[lo]] = w_stage[k[lo], 0, 1]
    M[p[~lo], p[~lo]] = w_stage[k[~lo], 1, 1]
    M[p[~lo], pl[~lo]] = w_stage[k[~lo], 1, 0]
    return M.T if transpose else M


def _chains(upw, dww):
    ua = np.zeros((7, 32, 128, 128), np.float32)
    ub = np.zeros((5, 32, 128, 128), np.float32)
    da = np.zeros((7, 32, 128, 128), np.float32)
    db = np.zeros((5, 32, 128, 128), np.float32)
    for c in range(32):
        ua[0, c] = _stage_block(upw[6], 6, c, False, True)
        for i, k in enumerate(range(5, -1, -1)):
            ua[1 + i, c] = _stage_block(upw[k], k, c, False, False)
        ub[0, c] = _stage_block(upw[11], 11, c, True, True)
        for i, f in enumerate(range(3, -1, -1)):
            ub[1 + i, c] = _stage_block(upw[7 + f], 7 + f, c, True, False)
        db[0, c] = _stage_block(dww[7], 7, c, True, False)
        for i, f in enumerate(range(1, 5)):
            db[1 + i, c] = _stage_block(dww[7 + f], 7 + f, c, True, True)
        da[0, c] = _stage_block(dww[0], 0, c, False, False)
        for i, k in enumerate(range(1, 7)):
            da[1 + i, c] = _stage_block(dww[k], k, c, False, True)
    def tr(a):
        return np.ascontiguousarray(a.transpose(1, 2, 0, 3).reshape(32, 128, -1))
    return tr(ua), tr(ub), tr(da), tr(db)


def _host_consts():
    pos = np.arange(T, dtype=np.float64)
    m = np.arange(0, DH, 2, dtype=np.float64) / DH
    th = pos[:, None] / (10000.0 ** m[None, :])
    crow = np.repeat(np.cos(th).T, 2, axis=0)
    srow = np.repeat(np.sin(th).T, 2, axis=0)
    CS = np.zeros((2, 128, T), np.float32)
    CS[0, 0:64] = crow; CS[0, 64:128] = crow
    CS[1, 0:64] = srow; CS[1, 64:128] = srow
    MASK = np.zeros((4, 128, 512), np.float32)
    r = np.arange(128)[:, None]
    cc = np.arange(512)[None, :]
    for p in range(4):
        MASK[p] = np.where(128 * p + r > cc, -1e9, 0.0)
    AUX = np.zeros((128, 512), np.float32)
    for j in range(32):
        AUX[2 * j + 1, 2 * j] = -1.0
        AUX[2 * j, 2 * j + 1] = 1.0
        AUX[64 + 2 * j + 1, 64 + 2 * j] = -1.0
        AUX[64 + 2 * j, 64 + 2 * j + 1] = 1.0
    AUX[r[:, 0], 128 + r[:, 0]] = 1.0
    for k in range(64):
        AUX[k, 256 + 64 + k] = 1.0
    AUX[:, 384:512] = 1.0
    CONSTB = np.zeros((128, 8), np.float32)
    CONSTB[:, 0] = EXP_SHIFT
    CONSTB[:, 1] = 1.0 / D
    CONSTB[:, 2] = 1e-5
    CONSTB[:, 3] = -1.0
    return CS, MASK, AUX, CONSTB


ENGS = ("TEN", "ACT", "DVE", "POOL", "SP")


DMA_DEPTH = 4
QSEMS = [f"qs{i}" for i in range(DMA_DEPTH)] + [f"qg{i}" for i in range(DMA_DEPTH)]
CSEMS = ["pe", "act", "dve", "pool"]


class Builder:

    def __init__(self, nc, prof=False):
        self.nc = nc
        self.prof = prof
        self.scope = "init"
        self.ops = {e: [] for e in ENGS}
        self.cnt = {s: 0 for s in CSEMS + QSEMS}
        self.qgrp = {"s": 0, "g": 0}

    def emit(self, eng, fn, waits=(), inc=None, n=1):
        flat = []
        for t in waits:
            if t is None:
                continue
            for pair in t:
                flat.append(pair)
        self.ops[eng].append((flat, fn, inc, n, self.scope))
        if inc is not None:
            self.cnt[inc] += n
        return ((inc, self.cnt[inc]),) if inc else None

    def mm(self, out, lhsT, rhs, start, stop, waits=()):
        return self.emit("TEN",
                         lambda e, o=out, l=lhsT, r=rhs, a=start, b=stop:
                         e.matmul(o, l, r, start=a, stop=b), waits, "pe")

    def act(self, out, in_, func, waits=(), **kw):
        return self.emit("ACT",
                         lambda e, o=out, i=in_, f=func, k=dict(kw):
                         e.activation(o, i, f, **k), waits, "act")

    def dve(self, fn, waits=()):
        return self.emit("DVE", fn, waits, "dve")

    def pool(self, fn, waits=()):
        return self.emit("POOL", fn, waits, "pool")

    def dma(self, q, out, in_, waits=(), transpose=False, slow=False):
        eng = {"s": "SP", "g": "POOL"}[q]
        sem = f"q{q}{self.qgrp[q] % DMA_DEPTH}"
        self.qgrp[q] += 1
        if transpose:
            fn = lambda e, o=out, i=in_: e.dma_start(out=o, in_=i, transpose=True)
        elif slow:
            fn = lambda e, o=out, i=in_: e.dma_start(out=o, in_=i,
                                                     allow_slow_non_contiguous=True)
        else:
            fn = lambda e, o=out, i=in_: e.dma_start(out=o, in_=i)
        drain = ((sem, self.cnt[sem]),)
        return self.emit(eng, fn, tuple(waits) + (drain,), sem, 16)

    def qtoks(self):
        return tuple((s, self.cnt[s]) for s in QSEMS)

    def realize(self, block, sems):
        def replay(engname, engobj):
            hwm = {}
            cur = [None]
            import contextlib as _cl
            stack = _cl.ExitStack()
            for waits, fn, inc, n, scope in self.ops[engname]:
                if self.prof and scope != cur[0]:
                    stack.close()
                    stack = _cl.ExitStack()
                    stack.enter_context(self.nc.named_scope(scope))
                    cur[0] = scope
                for s, v in waits:
                    if hwm.get(s, -1) < v:
                        engobj.wait_ge(sems[s], v)
                        hwm[s] = v
                inst = fn(engobj)
                if inc is not None:
                    inst.then_inc(sems[inc], n)
            stack.close()
        @block.tensor
        def _(e): replay("TEN", e)
        @block.scalar
        def _(e): replay("ACT", e)
        @block.vector
        def _(e): replay("DVE", e)
        @block.gpsimd
        def _(e): replay("POOL", e)
        @block.sync
        def _(e): replay("SP", e)


def last(toks):
    m = {}
    for t in toks:
        if t is None:
            continue
        for sem, v in t:
            if m.get(sem, -1) < v:
                m[sem] = v
    return tuple(m.items()) if m else None


def build_program(dbg=False, prof=False):
    nc = bass.Bass("TRN2", target_bir_lowering=False)
    dt = nc.dram_tensor
    xin = dt("xin", [NSEQ, T, D], F32, kind="ExternalInput").ap()
    wq = dt("wq", [D, D], F32, kind="ExternalInput").ap()
    wk = dt("wk", [D, D], F32, kind="ExternalInput").ap()
    wv = dt("wv", [D, D], F32, kind="ExternalInput").ap()
    wo = dt("wo", [D, D], F32, kind="ExternalInput").ap()
    astk = dt("astk", [D, 40], F32, kind="ExternalInput").ap()
    bstk = dt("bstk", [40, D], F32, kind="ExternalInput").ap()
    wexp = dt("wexp", [D, HID], F32, kind="ExternalInput").ap()
    wcon = dt("wcon", [HID, D], F32, kind="ExternalInput").ap()
    bexp = dt("bexp", [HID], F32, kind="ExternalInput").ap()
    upb = dt("upb", [HID], F32, kind="ExternalInput").ap()
    bcon = dt("bcon", [D], F32, kind="ExternalInput").ap()
    dwnb = dt("dwnb", [D], F32, kind="ExternalInput").ap()
    cs_c = dt("cs_c", [2, 128, T], BF16, kind="ExternalInput").ap()
    mask_c = dt("mask_c", [4, 128, 512], BF16, kind="ExternalInput").ap()
    aux_c = dt("aux_c", [128, 512], BF16, kind="ExternalInput").ap()
    cb_c = dt("cb_c", [128, 8], F32, kind="ExternalInput").ap()
    ch_ua = dt("ch_ua", [32, 128, 7 * 128], F32, kind="ExternalInput").ap()
    ch_ub = dt("ch_ub", [32, 128, 5 * 128], F32, kind="ExternalInput").ap()
    ch_da = dt("ch_da", [32, 128, 7 * 128], F32, kind="ExternalInput").ap()
    ch_db = dt("ch_db", [32, 128, 5 * 128], F32, kind="ExternalInput").ap()

    scrP = dt("scrP", [D, HID], BF16, kind="Internal").ap()
    scrP2 = dt("scrP2", [HID, D], BF16, kind="Internal").ap()
    scrP3 = dt("scrP3", [HID, D], BF16, kind="Internal").ap()
    scrW1 = dt("scrW1", [HID, D], BF16, kind="Internal").ap()
    scrW2 = dt("scrW2", [HID, D], BF16, kind="Internal").ap()
    scrT = dt("scrT", [T, D], BF16, kind="Internal").ap()
    scrS1 = dt("scrS1", [T, D], F32, kind="Internal").ap()
    scrS2 = dt("scrS2", [T, D], F32, kind="Internal").ap()
    scrV = dt("scrV", [HID], BF16, kind="Internal").ap()
    yout = dt("yout", [NSEQ, T, D], F32, kind="ExternalOutput").ap()

    B = Builder(nc, prof=prof)
    ctx = contextlib.ExitStack()
    E = ctx.enter_context
    NB = 94464
    NF = 5856
    ab_t = E(nc.sbuf_tensor("ab", [128, NB], BF16))
    af_t = E(nc.sbuf_tensor("af", [128, NF], F32))
    ps_t = E(nc.psum_tensor("ps", [128, 4096], F32))

    def bank(i, width=512, p0=0, p1=128):
        return ps_t[p0:p1, 512 * i:512 * i + width]

    o_aux = 0
    o_bs = 512
    o_otmp = 1536
    o_b2 = 2048
    PERS = 4096
    o_xf = PERS
    o_pt = o_xf
    o_drow = o_xf + 5120
    o_drow2 = o_xf + 7168
    o_mask = o_xf + 9216
    o_zt2 = o_xf + 11264
    o_stash = o_xf + 13312
    o_stash2 = o_xf + 15104
    o_bx = o_xf + 15616
    o_qb = PERS + 16384
    o_kb = PERS + 32768
    o_vb = PERS + 49152
    o_acat = PERS + 65792
    o_ust = o_acat
    o_vst = o_acat + 2048
    o_cs = o_acat + 4096
    o_as = o_acat + 8192
    o_tqv = o_acat + 8512
    o_w = PERS + 82176
    assert o_w + 8192 <= NB


    fo_cb = 0
    fo_den = 8
    fo_dinvb = 1032
    fo_x0 = 1544
    fo_s = 3592
    fo_tmp = 4616
    fo_sm = 5640; fo_sq = 5656; fo_rstd = 5672; fo_nmr = 5688
    fo_sm2 = 5704; fo_sq2 = 5720; fo_rstd2 = 5736; fo_nmr2 = 5752
    fo_c1 = 5768
    fo_b1 = 5776
    fo_bb = 5808
    assert 5840 <= NF

    AUXB = ab_t[:, o_aux:o_aux + 512]
    JT = AUXB[:, 0:128]
    I128 = AUXB[:, 128:256]
    SHIFT = ab_t[0:64, o_aux + 256:o_aux + 384]
    ONES1_64 = ab_t[64:65, o_aux + 384:o_aux + 448]
    ONESROW = ab_t[0:1, o_aux + 384:o_aux + 512]
    MASKB = lambda p: ab_t[:, o_mask + 512 * p:o_mask + 512 * p + 512]
    CB = lambda i: af_t[:, fo_cb + i:fo_cb + i + 1]
    CSC = lambda off, n: ab_t[:, o_cs + off:o_cs + off + n]
    XF = lambda c: ab_t[:, o_xf + 2048 * c:o_xf + 2048 * c + 2048]
    QB = lambda c: ab_t[:, o_qb + 2048 * c:o_qb + 2048 * c + 2048]
    KB = lambda c: ab_t[:, o_kb + 2048 * c:o_kb + 2048 * c + 2048]
    VE = lambda ts: ab_t[:, o_vb + 1040 * ts:o_vb + 1040 * ts + 1040]
    ACAT = lambda c: ab_t[:, o_acat + 2048 * c:o_acat + 2048 * c + 2048]
    WB = lambda c: ab_t[:, o_w + 1024 * c:o_w + 1024 * c + 1024]
    ASB = lambda c: ab_t[:, o_as + 40 * c:o_as + 40 * c + 40]
    PT = lambda kc: ab_t[:, o_pt + 512 * kc:o_pt + 512 * kc + 512]
    ZT = lambda c: ab_t[:, o_z + 1024 * c:o_z + 1024 * c + 1024]
    STASH = lambda i: ab_t[:, o_stash + 128 * i:o_stash + 128 * i + 128]
    RS = lambda i: ab_t[:, o_stash2 + 128 * i:o_stash2 + 128 * i + 128]
    BXC = lambda c: ab_t[:, o_bx + c:o_bx + c + 1]
    B2HI = lambda di: ab_t[0:1, o_b2 + 512 * di:o_b2 + 512 * di + 512]
    B2LO = lambda di: ab_t[0:1, o_b2 + 1024 + 512 * di:o_b2 + 1024 + 512 * di + 512]
    W1T = lambda dc: ab_t[:, o_qb + 4096 * dc:o_qb + 4096 * dc + 4096]
    W2T = lambda hc: ab_t[:, o_vb + 1024 * hc:o_vb + 1024 * hc + 1024]
    HACT = lambda i: ab_t[:, o_w + 256 * i:o_w + 256 * i + 256]
    SC5 = ab_t[:, o_zt2:o_zt2 + 1024]
    JUNK6 = ab_t[:, o_w + 1024:o_w + 2048]

    tk_aux = B.dma("s", AUXB, aux_c)
    tk_cb = B.dma("s", af_t[:, fo_cb:fo_cb + 8], cb_c)
    tk_bs = B.dma("g", ab_t[0:40, o_bs:o_bs + 1024], bstk)

    B.scope = 'prep'

    def pi_view(t, cs=None):
        v = t.rearrange("(a b c) f -> c b a f", a=32, b=4, c=32)
        return v if cs is None else v[cs]

    ZT2 = lambda i: ab_t[:, o_zt2 + 1024 * i:o_zt2 + 1024 * i + 1024]
    STASH = lambda q, i: ab_t[:, o_stash + 896 * q + 128 * i:
                              o_stash + 896 * q + 128 * i + 128]
    RS = lambda i: ab_t[:, o_stash2 + 128 * i:o_stash2 + 128 * i + 128]
    BXC = lambda c: ab_t[:, o_bx + c:o_bx + c + 1]

    tk_bc = B.dma("s", af_t[0:1, fo_s:fo_s + 1024], bcon.rearrange("(o d) -> o d", o=1))
    tk_db_ = B.dma("s", af_t[0:1, fo_tmp:fo_tmp + 1024], dwnb.rearrange("(o d) -> o d", o=1))
    tk_b2 = B.dve(lambda e: e.tensor_tensor(out=af_t[0:1, fo_s:fo_s + 1024],
                                            in0=af_t[0:1, fo_s:fo_s + 1024],
                                            in1=af_t[0:1, fo_tmp:fo_tmp + 1024],
                                            op=ALU.add), waits=(tk_bc, tk_db_))
    tk_b2h = B.dve(lambda e: e.tensor_copy(out=ab_t[0:1, o_b2:o_b2 + 1024],
                                           in_=af_t[0:1, fo_s:fo_s + 1024]), waits=(tk_b2,))
    tk_b2l = B.dve(lambda e: e.tensor_tensor(out=ab_t[0:1, o_b2 + 1024:o_b2 + 2048],
                                             in0=af_t[0:1, fo_s:fo_s + 1024],
                                             in1=ab_t[0:1, o_b2:o_b2 + 1024],
                                             op=ALU.subtract), waits=(tk_b2h,))

    tk_stage = [B.dma("g", ab_t[:, o_xf + 4096 * rg:o_xf + 4096 * rg + 4096],
                      wexp[128 * rg:128 * rg + 128, :]) for rg in range(8)]
    tk_wstage = [B.dma("s", scrP[128 * rg:128 * rg + 128, :],
                       ab_t[:, o_xf + 4096 * rg:o_xf + 4096 * rg + 4096],
                       waits=(tk_stage[rg],)) for rg in range(8)]
    tok_wstage = last(tk_wstage)
    tk_upb = B.dma("s", af_t[:, fo_bb:fo_bb + 32],
                   upb.rearrange("(a b c) -> b a c", a=32, b=4, c=32))

    prep_fin = {"w1": None, "w2": None, "dve": None, "b1": None}

    def _prep_gen(bank_gate):
        war = {0: bank_gate}
        stash_tok = {0: bank_gate, 1: bank_gate}
        zt_tok = {0: bank_gate, 1: bank_gate}
        chunk_res = [None]
        tk_bx = B.dma("g", ab_t[:, o_bx:o_bx + 32],
                      bexp.rearrange("(c p) -> p c", p=128), slow=True,
                      waits=(bank_gate,))

        def chunk(ch_dram, nmat, cs, load_fn, store_fn, bcol, tok_b):
            q = cs % 2
            bk_ = 2
            tkz = load_fn(ZT2(q), (zt_tok[q],))
            tk_st = B.dma("g", ab_t[:, o_stash + 896 * q:o_stash + 896 * q + 128 * nmat],
                          ch_dram[cs], waits=(stash_tok[q],))
            yield
            cur = STASH(q, 0)
            tk_prev = tk_st
            for i in range(1, nmat):
                tkm = B.mm(bank(bk_, 128), STASH(q, i), cur, True, True,
                           waits=(tk_st, tk_prev, war[0]))
                dstr = RS(2 * q + (i % 2))
                tk_prev = B.dve(lambda e, d=dstr, s_=bank(bk_, 128):
                                e.tensor_copy(out=d, in_=s_), waits=(tkm,))
                cur = dstr
                yield
            t1 = B.mm(bank(bk_), cur, ZT2(q)[:, 0:512], True, True,
                      waits=(tkz, tk_prev, war[0]))
            e1 = B.dve(lambda e, d=ZT2(q)[:, 0:512], s_=bank(bk_):
                       e.tensor_copy(out=d, in_=s_), waits=(t1,))
            yield
            t2 = B.mm(bank(bk_), cur, ZT2(q)[:, 512:1024], True, True, waits=(e1,))
            e2 = B.dve(lambda e, d=ZT2(q)[:, 512:1024], s_=bank(bk_):
                       e.tensor_copy(out=d, in_=s_), waits=(t2,))
            yield
            tb = None
            last_pe = t2
            if bcol is not None:
                t3 = B.mm(bank(bk_, 1), cur, bcol, True, True, waits=(tok_b, e2))
                tb = B.dve(lambda e, d=bcol, s_=bank(bk_, 1):
                           e.tensor_copy(out=d, in_=s_), waits=(t3,))
                last_pe = t3
                yield
            war[0] = last([e1, e2, tb])
            stash_tok[q] = last_pe
            tks = store_fn(ZT2(q), (last([e1, e2]),))
            zt_tok[q] = tks
            chunk_res[0] = (tks, last([e1, e2, tb]))

        ua_st = []
        ua_b = []
        for hc in range(32):
            yield from chunk(
                ch_ua, 7, hc,
                lambda d, w, hc=hc: B.dma("s", d, scrP[0:1024, 128 * hc:128 * hc + 128],
                                          waits=w + (tok_wstage,), transpose=True),
                lambda t, w, hc=hc: B.dma("s", scrP2[128 * hc:128 * hc + 128, :], t, waits=w),
                BXC(hc), tk_bx)
            tks, tball = chunk_res[0]
            ua_st.append(tks)
            ua_b.append(tball)
            yield
        tok_ua = last(ua_st)
        tkv1 = B.dma("s", scrV.rearrange("(c p) -> p c", p=128),
                     ab_t[:, o_bx:o_bx + 32], waits=(tok_ua, last(ua_b)), slow=True)
        tkv2 = B.dma("s", ab_t[:, o_bx:o_bx + 32],
                     scrV.rearrange("(a b c) -> b a c", a=32, b=4, c=32), waits=(tkv1,))
        vb_st = []
        vb_b = []
        for cs in range(32):
            yield from chunk(
                ch_ub, 5, cs,
                lambda d, w, cs=cs: B.dma("s", d, pi_view(scrP2, cs), waits=w + (tok_ua,)),
                lambda t, w, cs=cs: B.dma("s", scrW1[128 * cs:128 * cs + 128, :], t, waits=w),
                BXC(cs), tkv2)
            tks, tball = chunk_res[0]
            vb_st.append(tks)
            vb_b.append(tball)
            yield
        prep_fin["w1"] = last(vb_st)
        prep_fin["b1"] = B.dve(
            lambda e: e.tensor_tensor(out=af_t[:, fo_b1:fo_b1 + 32],
                                      in0=af_t[:, fo_bb:fo_bb + 32],
                                      in1=ab_t[:, o_bx:o_bx + 32], op=ALU.add),
            waits=(last(vb_b), tk_upb))
        w2m_st = []
        for cs in range(32):
            yield from chunk(
                ch_db, 5, cs,
                lambda d, w, cs=cs: B.dma("g", d, pi_view(wcon, cs), waits=w),
                lambda t, w, cs=cs: B.dma("s", scrP3[128 * cs:128 * cs + 128, :], t, waits=w),
                None, None)
            w2m_st.append(chunk_res[0][0])
            yield
        tok_w2m = last(w2m_st)
        w2_st = []
        for hc in range(32):
            yield from chunk(
                ch_da, 7, hc,
                lambda d, w, hc=hc: B.dma("s", d, pi_view(scrP3, hc), waits=w + (tok_w2m,)),
                lambda t, w, hc=hc: B.dma("s", pi_view(scrW2, hc), t, waits=w),
                None, None)
            w2_st.append(chunk_res[0][0])
            yield
        prep_fin["w2"] = last(w2_st)
        while True:
            yield

    B1C = lambda hc: af_t[:, fo_b1 + hc:fo_b1 + hc + 1]
    B2HI = lambda di: ab_t[0:1, o_b2 + 512 * di:o_b2 + 512 * di + 512]
    B2LO = lambda di: ab_t[0:1, o_b2 + 1024 + 512 * di:o_b2 + 1024 + 512 * di + 512]

    ones_dest = ab_t[:, o_vb:o_vb + 16640].rearrange(
        "p (t g s) -> p t g s", t=16, g=16, s=65)[:, :, :, 64:65]

    y_toks = []
    prev_den = {0: None, 1: None}
    prev_seq_gate = (tk_bs,)

    def emit_p2(sq, gates):
        B.scope = f'p2_{sq}'
        tk_x = B.dma("g", ab_t[:, o_qb:o_qb + 16384],
                     xin[sq].rearrange("(t p) f -> p t f", p=128), waits=gates)
        tk_xs = B.dma("s", scrT.rearrange("(t p) f -> p t f", p=128),
                      ab_t[:, o_qb:o_qb + 16384], waits=(tk_x,) + gates)
        return last([B.dma("s", XF(dc), scrT[:, 128 * dc:128 * dc + 128],
                           waits=(tk_xs,), transpose=True) for dc in range(8)])

    p2tok = {0: emit_p2(0, (tok_wstage, tk_bs))}
    p2gate = None
    p3_gate = (tk_bs,)
    w_gate = (tk_bs,)
    bank_gate = None
    p7gen = None
    p7state = {}
    for sq in range(NSEQ):
        tk_vones = B.pool(lambda e, d=ones_dest: e.memset(d, 1.0),
                          waits=p3_gate)
        if sq not in p2tok:
            p2tok[sq] = emit_p2(sq, p2gate)
        tok_xf = p2tok[sq]

        B.scope = f'p3_{sq}'
        tk_cs = B.dma("s", CSC(0, 2 * T), cs_c.rearrange("n p f -> p n f"),
                      waits=p3_gate)
        tk_as = B.dma("g", ab_t[:, o_as:o_as + 320],
                      astk.rearrange("(c p) r -> p c r", p=128), waits=p3_gate)
        tok_tqv = []
        for tt in range(4):
            tprev = None
            for c in range(8):
                tprev = B.mm(bank(6, 512, 0, 40), ASB(c), XF(c)[:, 512 * tt:512 * tt + 512],
                             c == 0, c == 7,
                             waits=(tok_xf, tk_as, tk_bs, bank_gate,
                                    tok_tqv[tt - 1] if tt else None) if c == 0 else ())
            tok_tqv.append(B.act(ab_t[0:40, o_tqv + 512 * tt:o_tqv + 512 * tt + 512],
                                 bank(6, 512, 0, 40), AF.Identity, scale=4.0,
                                 waits=(tprev,)))

        tk_wv = B.dma("g", ab_t[:, o_w:o_w + 8192],
                      wv.rearrange("(c p) f -> p c f", p=128), waits=w_gate)
        last_evac = {}
        for u, (ts, fo) in enumerate([(a, b) for a in range(16) for b in range(2)]):
            bk = u % 4
            tprev = None
            for c in range(8):
                tprev = B.mm(bank(bk), XF(c)[:, 128 * ts:128 * ts + 128],
                             WB(c)[:, 512 * fo:512 * fo + 512], c == 0, False,
                             waits=(tk_wv, tok_tqv[ts // 4], last_evac.get(bk))
                             if c == 0 else ())
            tprev = B.mm(bank(bk), ab_t[32:40, o_tqv + 128 * ts:o_tqv + 128 * ts + 128],
                         ab_t[32:40, o_bs + 512 * fo:o_bs + 512 * fo + 512],
                         False, True, waits=())
            vdst = VE(ts).rearrange("p (g s) -> p g s", g=16, s=65)[:, 8 * fo:8 * fo + 8, 0:64]
            last_evac[bk] = B.dve(lambda e, d=vdst, s=bank(bk): e.tensor_copy(out=d, in_=s),
                                  waits=(tprev,))
        tok_v = last(list(last_evac.values()))

        def proj_rope(wdram, dst, lora, wwait, drive=None):
            tk_w = B.dma("g", ab_t[:, o_w:o_w + 8192],
                         wdram.rearrange("(c p) f -> p c f", p=128), waits=(wwait,))
            lastu, lastj, lastq, lastjb = {}, {}, {}, {}
            toks = []
            for u, (co, tt) in enumerate([(a, b) for a in range(8) for b in range(4)]):
                bk = u % 4
                jb = 4 + (u % 4)
                tprev = None
                for c in range(8):
                    tprev = B.mm(bank(bk), WB(c)[:, 128 * co:128 * co + 128],
                                 XF(c)[:, 512 * tt:512 * tt + 512], c == 0,
                                 (c == 7) and not lora,
                                 waits=(tk_w, lastu.get(bk),
                                        tok_tqv[tt] if lora else None) if c == 0 else ())
                if lora:
                    tprev = B.mm(bank(bk), ab_t[0:8, o_bs + 128 * co:o_bs + 128 * co + 128],
                                 ab_t[0:8, o_tqv + 512 * tt:o_tqv + 512 * tt + 512],
                                 False, True, waits=())
                sl = u % 4
                us = ab_t[:, o_ust + 512 * sl:o_ust + 512 * sl + 512]
                vs = ab_t[:, o_vst + 512 * sl:o_vst + 512 * sl + 512]
                tu = B.dve(lambda e, d=us, s=bank(bk), cs_=CSC(T + 512 * tt, 512):
                           e.tensor_tensor(out=d, in0=s, in1=cs_, op=ALU.mult),
                           waits=(tprev, lastj.get(sl), tk_cs))
                tj = B.mm(bank(jb), JT, us, True, True,
                          waits=(tu, tk_aux, lastjb.get(jb, bank_gate)))
                tv = B.dve(lambda e, d=vs, s=bank(bk), cs_=CSC(512 * tt, 512):
                           e.tensor_tensor(out=d, in0=s, in1=cs_, op=ALU.mult),
                           waits=(tprev, lastj.get(sl)))
                lastu[bk] = tv
                dd = dst(co)[:, 512 * tt:512 * tt + 512]
                tr = B.dve(lambda e, d=dd, a=vs, b=bank(jb):
                           e.tensor_tensor(out=d, in0=b, in1=a, op=ALU.add),
                           waits=(tv, tj))
                lastj[sl] = tr
                lastjb[jb] = tr
                toks.append(tr)
                if drive:
                    drive()
            return last(toks)

        drvn = [0]

        def drv():
            drvn[0] += 1
            if p7gen is not None and drvn[0] % 2 == 0:
                next(p7gen, None)

        tok_q = proj_rope(wq, QB, True, tok_v, drive=drv)
        tok_k = proj_rope(wk, KB, False, tok_q, drive=drv)
        if p7gen is not None:
            for _ in p7gen:
                pass
            p7gen = None
        if sq > 0:
            prev_seq_gate = p7state['gate']
            prev_den = {h: last([prev_den[h], prev_seq_gate[0]]) for h in (0, 1)}

        B.scope = f'p4_{sq}'
        tk_mask = B.dma("s", ab_t[:, o_mask:o_mask + 2048],
                        mask_c.rearrange("n p f -> p n f"), waits=(tok_k,))
        if sq == 0:
            prep_gen = _prep_gen(tok_k)
        tok_acat = []
        last_sbank = {}
        last_ptg = {}
        last_pv = {}
        last_dbank = {}
        last_dinvb = {}
        last_otmp = [None]
        pv_tok = {}
        pend = []
        dgens = []

        def drain_dg():
            while dgens:
                try:
                    next(dgens[0])
                except StopIteration:
                    dgens.pop(0)

        def drive_dg(n):
            done = 0
            while dgens and done < n:
                try:
                    next(dgens[0])
                    done += 1
                except StopIteration:
                    dgens.pop(0)

        def emit_pv(s, hp, j, kc, te0, te1):
            if kc == 0 and j in (0, 2):
                drain_dg()
            for half, te in ((0, te0), (1, te1)):
                pb = 4 + 2 * half + (j % 2)
                pv = B.mm(bank(pb, 512, 0, 65),
                          VE(kc)[:, 65 * (2 * hp + half):65 * (2 * hp + half) + 65],
                          ab_t[:, o_pt + 1024 * s + 512 * half:
                               o_pt + 1024 * s + 512 * half + 512],
                          kc == 0, kc == 4 * j + 3,
                          waits=(te, last_pv.get((half, j % 2)), tok_v, tk_vones)
                          if kc == 0 else (te,))
                if kc == 4 * j + 3:
                    pv_tok[(hp, half, j)] = pv
                last_ptg[(s, half)] = pv

        def den_gen(hp, j):
            while ((hp, 0, j) not in pv_tok) or ((hp, 1, j) not in pv_tok):
                yield
            for half in (0, 1):
                pbase = 2048 + 1024 * half
                fo_dh = fo_den if half == 0 else fo_x0
                o_drh = o_drow if half == 0 else o_drow2
                tln = B.act(af_t[64:65, fo_dh:fo_dh + 1024],
                            ps_t[64:65, pbase:pbase + 1024],
                            AF.Ln, waits=(pv_tok[(hp, half, j - 1)],
                                          pv_tok[(hp, half, j)], prev_den[half]))
                yield
                tex = B.act(af_t[64:65, fo_dh:fo_dh + 1024],
                            af_t[64:65, fo_dh:fo_dh + 1024],
                            AF.Exp, scale=-1.0, waits=(tln,))
                yield
                thi = B.dve(lambda e, o=ab_t[64:65, o_drh:o_drh + 1024],
                            i=af_t[64:65, fo_dh:fo_dh + 1024]:
                            e.tensor_copy(out=o, in_=i),
                            waits=(tex, prev_den[half]))
                yield
                tlo = B.dve(lambda e, o=ab_t[64:65, o_drh + 1024:o_drh + 2048],
                            a=af_t[64:65, fo_dh:fo_dh + 1024],
                            b=ab_t[64:65, o_drh:o_drh + 1024]:
                            e.tensor_tensor(out=o, in0=a, in1=b,
                                            op=ALU.subtract), waits=(thi,))
                yield
                for jj in range(2):
                    jfull = j - 1 + jj
                    db_ = 3
                    pb = 4 + 2 * half + (jfull % 2)
                    tb1 = B.mm(bank(db_, 512, 0, 64), ONES1_64,
                               ab_t[64:65, o_drh + 512 * jj:
                                    o_drh + 512 * jj + 512],
                               True, False,
                               waits=(thi, last_dbank.get(db_)))
                    tb2 = B.mm(bank(db_, 512, 0, 64), ONES1_64,
                               ab_t[64:65, o_drh + 1024 + 512 * jj:
                                    o_drh + 1024 + 512 * jj + 512],
                               False, True, waits=(tlo,))
                    yield
                    tev = B.act(af_t[0:64, fo_dinvb + 512 * jj:
                                     fo_dinvb + 512 * jj + 512],
                                bank(db_, 512, 0, 64),
                                AF.Identity, scale=1.0,
                                waits=(tb2, last_dinvb.get(jj)))
                    last_dbank[db_] = tev
                    yield
                    dinvb = af_t[0:64, fo_dinvb + 512 * jj:
                                 fo_dinvb + 512 * jj + 512]
                    if half == 0:
                        dst = ACAT(hp)[0:64, 512 * jfull:512 * jfull + 512]
                        tn = B.dve(lambda e, d=dst, s=bank(pb, 512, 0, 64),
                                   v=dinvb:
                                   e.tensor_tensor(out=d, in0=s, in1=v,
                                                   op=ALU.mult), waits=(tev,))
                        last_dinvb[jj] = tn
                        last_pv[(half, jfull % 2)] = tn
                        prev_den[half] = tn
                        tok_acat.append(tn)
                        yield
                    else:
                        tn = B.dve(lambda e, s=bank(pb, 512, 0, 64), v=dinvb:
                                   e.tensor_tensor(
                                       out=ab_t[0:64, o_otmp:o_otmp + 512],
                                       in0=s, in1=v, op=ALU.mult),
                                   waits=(tev, last_otmp[0]))
                        yield
                        tsh = B.mm(bank(db_), SHIFT,
                                   ab_t[0:64, o_otmp:o_otmp + 512],
                                   True, True, waits=(tn,))
                        last_otmp[0] = tsh
                        yield
                        dst = ACAT(hp)[64:128, 512 * jfull:512 * jfull + 512]
                        tn2 = B.dve(lambda e, d=dst, s=bank(db_, 512, 64, 128):
                                    e.tensor_copy(out=d, in_=s), waits=(tsh,))
                        last_dbank[db_] = tn2
                        last_dinvb[jj] = tn2
                        last_pv[(half, jfull % 2)] = tn2
                        prev_den[half] = tn2
                        tok_acat.append(tn2)
                        yield

        it_ctr = [0]
        for hp in range(8):
            c = hp
            for j in range(4):
                for kc in range(4 * j + 4):
                    s = it_ctr[0] % 5
                    it_ctr[0] += 1
                    diag = kc >= 4 * j
                    mm0 = B.mm(bank(0),
                               KB(c)[0:64, 128 * kc:128 * kc + 128],
                               QB(c)[0:64, 512 * j:512 * j + 512],
                               True, not diag,
                               waits=(last_sbank.get(0), tok_k, tk_mask))
                    if diag:
                        mm0 = B.mm(bank(0), I128, MASKB(kc % 4),
                                   False, True, waits=())
                    te0 = B.act(ab_t[:, o_pt + 1024 * s:o_pt + 1024 * s + 512],
                                ps_t[:, 0:512], AF.Exp, scale=0.125, bias=CB(0),
                                waits=(mm0, last_ptg.get((s, 0))))
                    mm1 = B.mm(bank(1),
                               KB(c)[64:128, 128 * kc:128 * kc + 128],
                               QB(c)[64:128, 512 * j:512 * j + 512],
                               True, not diag,
                               waits=(last_sbank.get(1), tok_k, tk_mask))
                    if diag:
                        mm1 = B.mm(bank(1), I128, MASKB(kc % 4),
                                   False, True, waits=())
                    te1 = B.act(ab_t[:, o_pt + 1024 * s + 512:o_pt + 1024 * s + 1024],
                                ps_t[:, 512:1024], AF.Exp, scale=0.125, bias=CB(0),
                                waits=(mm1, last_ptg.get((s, 1))))
                    last_sbank[0] = te0
                    last_sbank[1] = te1
                    pend.append((s, hp, j, kc, te0, te1))
                    if len(pend) > 3:
                        emit_pv(*pend.pop(0))
                    if sq == 0:
                        for _ in range(4):
                            next(prep_gen)
                    drive_dg(3)
                while len(pend) > 2:
                    emit_pv(*pend.pop(0))
                if j % 2 == 1:
                    dgens.append(den_gen(hp, j))
        while pend:
            emit_pv(*pend.pop(0))
        drain_dg()
        if sq == 0:
            for _ in range(400):
                next(prep_gen)
        tok_acat = last(tok_acat)

        B.scope = f'p5_{sq}'
        tk_wo = B.dma("g", ab_t[:, o_w:o_w + 8192],
                      wo.rearrange("(c p) f -> p c f", p=128), waits=(tok_k,))
        S1T = af_t[:, fo_s:fo_s + 1024]
        tok_x1 = []
        x1_half = []
        wo_mms = []
        tk_x0 = {}
        ttdone = {}
        last_s1bank = {}
        prev_gate = None
        prev_ln = None
        tk_x0[0] = B.dma("s", af_t[:, fo_x0:fo_x0 + 1024], xin[sq, 0:128, :],
                         waits=prev_seq_gate + (prev_den[0], prev_den[1], tok_acat))
        tk_x0[1] = B.dma("s", af_t[:, fo_x0 + 1024:fo_x0 + 2048], xin[sq, 128:256, :],
                         waits=prev_seq_gate + (prev_den[0], prev_den[1], tok_acat))
        tok_w1 = last([B.dma("s", W1T(dc), scrW1[:, 128 * dc:128 * dc + 128],
                             waits=(tok_acat, prep_fin["w1"], prep_fin["b1"]),
                             transpose=True)
                       for dc in range(8)])
        tok_xfb = {}
        tk_w2p = []
        w2view = scrW2.rearrange("(c p) f -> p c f", p=128)
        for ts in range(16):
            if ts == 8:
                tok_xfb[0] = last([B.dma("s", XF(dc)[:, 0:1024],
                                         scrT[0:1024, 128 * dc:128 * dc + 128],
                                         waits=(last(x1_half), prep_fin["w2"]),
                                         transpose=True)
                                   for dc in range(8)])
            B.scope = f'p5mm_{sq}'
            tts = []
            for fo in range(2):
                bk = 2 * (ts % 2) + fo
                tprev = None
                for c in range(8):
                    tprev = B.mm(bank(bk), ACAT(c)[:, 128 * ts:128 * ts + 128],
                                 WB(c)[:, 512 * fo:512 * fo + 512], c == 0, c == 7,
                                 waits=(tok_acat, tk_wo, last_s1bank.get(bk),
                                        prep_fin["w1"], prep_fin["w2"])
                                 if c == 0 else ())
                if ts == 15:
                    wo_mms.append(tprev)
                x0s = af_t[:, fo_x0 + 1024 * (ts % 2) + 512 * fo:
                           fo_x0 + 1024 * (ts % 2) + 512 * fo + 512]
                tt = B.dve(lambda e, d=S1T[:, 512 * fo:512 * fo + 512], s=bank(bk), x=x0s:
                           e.tensor_tensor(out=d, in0=s, in1=x, op=ALU.add),
                           waits=(tprev, tk_x0[ts], prev_gate))
                last_s1bank[bk] = tt
                tts.append(tt)
            ttdone[ts] = last(tts)
            if ts == 15:
                for i in range(4):
                    tk_w2p.append(B.dma("sg"[i % 2],
                                        ab_t[:, o_vb + 8192 * i:o_vb + 8192 * i + 8192],
                                        w2view[:, 8 * i:8 * i + 8, :],
                                        waits=(wo_last := last(wo_mms), tok_acat,
                                               prep_fin["w2"])))
            if ts + 2 < 16:
                tk_x0[ts + 2] = B.dma("s",
                                      af_t[:, fo_x0 + 1024 * (ts % 2):fo_x0 + 1024 * (ts % 2) + 1024],
                                      xin[sq, 128 * (ts + 2):128 * (ts + 2) + 128, :],
                                      waits=(ttdone[ts],))
            B.scope = f'p5st_{sq}'
            tsq = B.act(SC5, S1T, AF.Square,
                        accum_out=af_t[:, fo_sq + ts:fo_sq + ts + 1],
                        waits=tuple(tts) + ((prep_fin["w2"],) if ts == 0 else ()))
            tsm = B.dve(lambda e, o=af_t[:, fo_sm + ts:fo_sm + ts + 1]:
                        e.tensor_reduce(out=o, in_=S1T, axis=AX.X, op=ALU.add),
                        waits=tuple(tts))
            B.scope = f'p5ln_{sq}'
            c1 = af_t[:, fo_c1:fo_c1 + 1]
            c2 = af_t[:, fo_c1 + 1:fo_c1 + 2]
            c3 = af_t[:, fo_c1 + 4:fo_c1 + 5]
            t1 = B.dve(lambda e, o=c1, i=af_t[:, fo_sm + ts:fo_sm + ts + 1]:
                       e.tensor_tensor(out=o, in0=i, in1=CB(1), op=ALU.mult),
                       waits=(tsm, prev_ln))
            t2 = B.dve(lambda e, o=c2, i=af_t[:, fo_sq + ts:fo_sq + ts + 1]:
                       e.tensor_tensor(out=o, in0=i, in1=CB(1), op=ALU.mult),
                       waits=(tsq, prev_ln))
            t3a = B.dve(lambda e, o=c3, b=c1:
                        e.tensor_tensor(out=o, in0=b, in1=b, op=ALU.mult), waits=(t1,))
            t3 = B.dve(lambda e, o=c2, a=c2, b=c3:
                       e.tensor_tensor(out=o, in0=a, in1=b, op=ALU.subtract),
                       waits=(t2, t3a))
            t4 = B.dve(lambda e, o=c2, a=c2:
                       e.tensor_tensor(out=o, in0=a, in1=CB(2), op=ALU.add), waits=(t3,))
            t5 = B.act(c2, c2, AF.Ln, waits=(t4,))
            t6 = B.act(af_t[:, fo_rstd + ts:fo_rstd + ts + 1], c2, AF.Exp, scale=-0.5,
                       waits=(t5,))
            t7 = B.dve(lambda e, o=af_t[:, fo_nmr + ts:fo_nmr + ts + 1], a=c1,
                       r=af_t[:, fo_rstd + ts:fo_rstd + ts + 1]:
                       e.scalar_tensor_tensor(out=o, in0=a, scalar=CB(3), in1=r,
                                              op0=ALU.mult, op1=ALU.mult), waits=(t6,))
            prev_ln = last([t7, t6])
            B.scope = f'p5sp_{sq}'
            tx1 = B.act(SC5, S1T, AF.Identity,
                        scale=af_t[:, fo_rstd + ts:fo_rstd + ts + 1],
                        bias=af_t[:, fo_nmr + ts:fo_nmr + ts + 1], waits=(t7,))
            tk_sp = B.dma("s", scrS1[128 * ts:128 * ts + 128, :], S1T, waits=(tsq, tsm))
            tk_x1 = B.dma("s", scrT[128 * ts:128 * ts + 128, :], SC5, waits=(tx1,))
            prev_gate = last([tk_x1, tk_sp])
            tok_x1 += [tk_x1, tk_sp]
            if ts <= 7:
                x1_half.append(tk_x1)
        tok_x1 = last(tok_x1)
        tok_x1h = last(x1_half)
        wo_last = last(wo_mms)
        s1bank_tok = dict(last_s1bank)

        B.scope = f'p6_{sq}'
        tok_xfb[1] = last([B.dma("s", XF(dc)[:, 1024:2048],
                                 scrT[1024:2048, 128 * dc:128 * dc + 128],
                                 waits=(tok_x1,), transpose=True)
                           for dc in range(8)])
        tk_w2l = last(tk_w2p)
        tk_s1r = {}
        sttdone = {}
        seq_stats = {}
        prev_s2sp = None
        last_gelu = {}
        last_w2use = {}
        last_obank = {}
        tk_s1r[0] = B.dma("s", af_t[:, fo_x0:fo_x0 + 1024], scrS1[0:128, :],
                          waits=(tok_x1,))
        tk_s1r[1] = B.dma("s", af_t[:, fo_x0 + 1024:fo_x0 + 2048], scrS1[128:256, :],
                          waits=(tok_x1,))
        for Bi in range(8):
            gelu_t = {}
            def w1_step(hc, Bi=Bi):
                pb = hc % 4
                tprev = None
                for cix in range(8):
                    tprev = B.mm(ps_t[:, 512 * pb:512 * pb + 256],
                                 W1T(cix)[:, 128 * hc:128 * hc + 128],
                                 XF(cix)[:, 256 * Bi:256 * Bi + 256], cix == 0, cix == 7,
                                 waits=(tok_w1, tok_xfb[Bi // 4],
                                        last_gelu.get(pb, s1bank_tok.get(pb)))
                                 if cix == 0 else ())
                tg = B.act(HACT(pb), ps_t[:, 512 * pb:512 * pb + 256], AF.Gelu,
                           bias=B1C(hc), waits=(tprev, last_w2use.get(pb)))
                last_gelu[pb] = tg
                gelu_t[hc] = tg
            def w2_step(hc):
                pb = hc % 4
                tg = gelu_t[hc]
                wprev = None
                for ti in range(2):
                    for di in range(2):
                        ob = 4 + 2 * ti + di
                        wprev = B.mm(bank(ob), HACT(pb)[:, 128 * ti:128 * ti + 128],
                                     W2T(hc)[:, 512 * di:512 * di + 512], hc == 0, False,
                                     waits=(tg, tk_w2l, last_obank.get(ob))
                                     if hc == 0 else (tg,))
                last_w2use[pb] = wprev
            w1_step(0)
            w1_step(1)
            for hc in range(2, 32):
                w1_step(hc)
                w2_step(hc - 2)
            w2_step(30)
            w2_step(31)
            for ti in range(2):
                ts = 2 * Bi + ti
                obank_toks = []
                for di in range(2):
                    ob = 4 + 2 * ti + di
                    B.mm(bank(ob), ONESROW, B2HI(di), False, False,
                         waits=(tk_b2l,))
                    tb = B.mm(bank(ob), ONESROW, B2LO(di), False, True, waits=())
                    s1s = af_t[:, fo_x0 + 1024 * (ts % 2) + 512 * di:
                               fo_x0 + 1024 * (ts % 2) + 512 * di + 512]
                    tt = B.dve(lambda e, o=af_t[:, fo_tmp + 512 * di:fo_tmp + 512 * di + 512],
                               a=s1s, r=af_t[:, fo_rstd + ts:fo_rstd + ts + 1], p=bank(ob):
                               e.scalar_tensor_tensor(out=o, in0=a, scalar=r, in1=p,
                                                      op0=ALU.mult, op1=ALU.add),
                               waits=(tb, tk_s1r[ts]))
                    last_obank[ob] = tt
                    obank_toks.append(tt)
                sttdone[ts] = last(obank_toks)
                if ts + 2 < 16:
                    tk_s1r[ts + 2] = B.dma("s",
                                           af_t[:, fo_x0 + 1024 * (ts % 2):fo_x0 + 1024 * (ts % 2) + 1024],
                                           scrS1[128 * (ts + 2):128 * (ts + 2) + 128, :],
                                           waits=(sttdone[ts],))
                ts2 = B.act(af_t[:, fo_s:fo_s + 1024], af_t[:, fo_tmp:fo_tmp + 1024],
                            AF.Identity, scale=1.0,
                            bias=af_t[:, fo_nmr + ts:fo_nmr + ts + 1],
                            waits=(sttdone[ts], prev_s2sp))
                tsq2 = B.act(JUNK6, af_t[:, fo_s:fo_s + 1024], AF.Square,
                             accum_out=af_t[:, fo_sq2 + ts:fo_sq2 + ts + 1], waits=(ts2,))
                tsm2 = B.dve(lambda e, o=af_t[:, fo_sm2 + ts:fo_sm2 + ts + 1]:
                             e.tensor_reduce(out=o, in_=af_t[:, fo_s:fo_s + 1024],
                                             axis=AX.X, op=ALU.add), waits=(ts2,))
                tk_s2 = B.dma("s", scrS2[128 * ts:128 * ts + 128, :],
                              af_t[:, fo_s:fo_s + 1024], waits=(tsq2, tsm2))
                prev_s2sp = tk_s2
                seq_stats[ts] = (tk_s2, tsq2, tsm2)

        def p7_stream(sq=sq, seq_stats=seq_stats, sttdone=sttdone):
            tk_s2r = {}
            ty_prev = None
            ty_gates = {}
            prev_ln2 = None
            seq_toks = []
            for ts in range(16):
                yield from p7_ts(sq, ts, seq_stats, sttdone, tk_s2r, ty_gates,
                                 seq_toks, st := {'ty_prev': ty_prev,
                                                  'prev_ln2': prev_ln2})
                ty_prev = st['ty_prev']
                prev_ln2 = st['prev_ln2']
            p7state['gate'] = (last(seq_toks),)

        def p7_ts(sq, ts, seq_stats, sttdone, tk_s2r, ty_gates, seq_toks, st):
            ty_prev = st['ty_prev']
            prev_ln2 = st['prev_ln2']
            tk_s2, tsq2, tsm2 = seq_stats[ts]
            gate = tk_s2r.get(ts - 2)
            if ts in (0, 1):
                gate = sttdone[14 + ts]
            tk_s2r[ts] = B.dma("s",
                               af_t[:, fo_x0 + 1024 * (ts % 2):fo_x0 + 1024 * (ts % 2) + 1024],
                               scrS2[128 * ts:128 * ts + 128, :],
                               waits=(tk_s2, gate) if ts < 2 else
                               (tk_s2, ty_gates[ts - 2]))
            c1 = af_t[:, fo_c1 + 2:fo_c1 + 3]
            c2 = af_t[:, fo_c1 + 3:fo_c1 + 4]
            c3 = af_t[:, fo_c1 + 5:fo_c1 + 6]
            t1 = B.dve(lambda e, o=c1, i=af_t[:, fo_sm2 + ts:fo_sm2 + ts + 1]:
                       e.tensor_tensor(out=o, in0=i, in1=CB(1), op=ALU.mult),
                       waits=(tsm2, prev_ln2))
            t2 = B.dve(lambda e, o=c2, i=af_t[:, fo_sq2 + ts:fo_sq2 + ts + 1]:
                       e.tensor_tensor(out=o, in0=i, in1=CB(1), op=ALU.mult),
                       waits=(tsq2, prev_ln2))
            t3a = B.dve(lambda e, o=c3, b=c1:
                        e.tensor_tensor(out=o, in0=b, in1=b, op=ALU.mult), waits=(t1,))
            t3 = B.dve(lambda e, o=c2, a=c2, b=c3:
                       e.tensor_tensor(out=o, in0=a, in1=b, op=ALU.subtract),
                       waits=(t2, t3a))
            t4 = B.dve(lambda e, o=c2, a=c2:
                       e.tensor_tensor(out=o, in0=a, in1=CB(2), op=ALU.add), waits=(t3,))
            t5 = B.act(c2, c2, AF.Ln, waits=(t4,))
            t6 = B.act(af_t[:, fo_rstd2 + ts:fo_rstd2 + ts + 1], c2, AF.Exp, scale=-0.5,
                       waits=(t5,))
            t7 = B.dve(lambda e, o=af_t[:, fo_nmr2 + ts:fo_nmr2 + ts + 1], a=c1,
                       r=af_t[:, fo_rstd2 + ts:fo_rstd2 + ts + 1]:
                       e.scalar_tensor_tensor(out=o, in0=a, scalar=CB(3), in1=r,
                                              op0=ALU.mult, op1=ALU.mult), waits=(t6,))
            prev_ln2 = last([t7, t6])
            ty = B.act(af_t[:, fo_tmp:fo_tmp + 1024],
                       af_t[:, fo_x0 + 1024 * (ts % 2):fo_x0 + 1024 * (ts % 2) + 1024],
                       AF.Identity, scale=af_t[:, fo_rstd2 + ts:fo_rstd2 + ts + 1],
                       bias=af_t[:, fo_nmr2 + ts:fo_nmr2 + ts + 1],
                       waits=(t7, tk_s2r[ts], ty_prev))
            ty_gates[ts] = ty
            tk_y = B.dma("s", yout[sq, 128 * ts:128 * ts + 128, :],
                         af_t[:, fo_tmp:fo_tmp + 1024], waits=(ty,))
            st['ty_prev'] = tk_y
            st['prev_ln2'] = prev_ln2
            seq_toks.append(tk_y)
            y_toks.append(tk_y)
            yield

        if HOIST:
            bank_gate = last(list(last_obank.values()))
            gate_s4 = last(list(last_w2use.values()))
            w_gate = (last([gate_s4, seq_stats[15][1]]),)
            p3_gate = (gate_s4,)
            if sq + 1 < NSEQ:
                p2tok[sq + 1] = emit_p2(sq + 1, (last(list(last_gelu.values())),))
            p7gen = p7_stream()
            if sq + 1 == NSEQ:
                for _ in p7gen:
                    pass
                p7gen = None
        else:
            for _ in p7_stream():
                pass
            prev_seq_gate = p7state['gate']
            p3_gate = prev_seq_gate
            w_gate = prev_seq_gate
            p2gate = (last(list(last_gelu.values())),)

    B.emit("SP", lambda e: e.nop(), waits=(B.qtoks(),))

    sems = {}
    for s in CSEMS + QSEMS:
        sems[s] = E(nc.semaphore(s))
    block = E(nc.Block())
    B.realize(block, sems)
    ctx.close()
    return nc


def kernel(**inputs):
    x = _f32(inputs["L0_attn_query"])
    CS, MASK, AUX, CONSTB = _host_consts()
    ua, ub, da, db = _chains(_f32(inputs["L0_ffn_up_weights"]),
                             _f32(inputs["L0_ffn_down_weights"]))
    astk = np.zeros((D, 40), np.float32)
    astk[:, 0:8] = _f32(inputs["L0_attn_Aq"])
    astk[:, 32:40] = _f32(inputs["L0_attn_Av"])
    bstk = np.zeros((40, D), np.float32)
    bstk[0:8, :] = _f32(inputs["L0_attn_Bq"])
    bstk[32:40, :] = _f32(inputs["L0_attn_Bv"])

    common = {
        "wq": _f32(inputs["L0_attn_wq"]), "wk": _f32(inputs["L0_attn_wk"]),
        "wv": _f32(inputs["L0_attn_wv"]), "wo": _f32(inputs["L0_attn_wo"]),
        "astk": astk, "bstk": bstk,
        "wexp": _f32(inputs["L0_ffn_W_exp"]), "wcon": _f32(inputs["L0_ffn_W_con"]),
        "bexp": _f32(inputs["L0_ffn_b_exp"]), "upb": _f32(inputs["L0_ffn_up_bias"]),
        "bcon": _f32(inputs["L0_ffn_b_con"]), "dwnb": _f32(inputs["L0_ffn_down_bias"]),
        "cs_c": _bf(CS), "mask_c": _bf(MASK), "aux_c": _bf(AUX), "cb_c": CONSTB,
        "ch_ua": ua, "ch_ub": ub, "ch_da": da, "ch_db": db,
    }
    nc = build_program()
    in_maps = []
    for c in range(NCORES):
        m = dict(common)
        m["xin"] = np.ascontiguousarray(x[2 * c:2 * c + 2])
        in_maps.append(m)
    res = run_bass_kernel_spmd(nc, in_maps, core_ids=list(range(NCORES)))
    return np.concatenate([r["yout"] for r in res.results], axis=0)


# version 17
# speedup vs baseline: 1.0082x; 1.0082x over previous
import numpy as np
import ml_dtypes
import contextlib
import concourse.bass as bass
import concourse.mybir as mybir
from concourse.bass_utils import run_bass_kernel_spmd

F32 = mybir.dt.float32
BF16 = mybir.dt.bfloat16
AF = mybir.ActivationFunctionType
ALU = mybir.AluOpType
AX = mybir.AxisListType

HOIST = True
D, T, H, DH, HID = 1024, 2048, 16, 64, 4096
NSEQ = 2
NCORES = 8
EXP_SHIFT = -10.0

_bf = lambda a: np.ascontiguousarray(a).astype(ml_dtypes.bfloat16)
_f32 = lambda a: np.ascontiguousarray(a, dtype=np.float32)


def _swap_bits(i):
    return ((i & 0x1F) << 7) | (i & 0x60) | ((i >> 7) & 0x1F)

_PI = np.array([_swap_bits(i) for i in range(HID)])


def _stage_block(w_stage, s, chunk, perm, transpose):
    sig = 1 << s
    M = np.zeros((128, 128), np.float32)
    base = 128 * chunk
    p = np.arange(128)
    gi = _PI[base + p] if perm else base + p
    pl = p ^ (1 << (s - 7)) if perm else p ^ sig
    k = ((gi & ~(2 * sig - 1)) >> 1) | (gi & (sig - 1))
    lo = (gi & sig) == 0
    M[p[lo], p[lo]] = w_stage[k[lo], 0, 0]
    M[p[lo], pl[lo]] = w_stage[k[lo], 0, 1]
    M[p[~lo], p[~lo]] = w_stage[k[~lo], 1, 1]
    M[p[~lo], pl[~lo]] = w_stage[k[~lo], 1, 0]
    return M.T if transpose else M


def _chains(upw, dww):
    ua = np.zeros((7, 32, 128, 128), np.float32)
    ub = np.zeros((5, 32, 128, 128), np.float32)
    da = np.zeros((7, 32, 128, 128), np.float32)
    db = np.zeros((5, 32, 128, 128), np.float32)
    for c in range(32):
        ua[0, c] = _stage_block(upw[6], 6, c, False, True)
        for i, k in enumerate(range(5, -1, -1)):
            ua[1 + i, c] = _stage_block(upw[k], k, c, False, False)
        ub[0, c] = _stage_block(upw[11], 11, c, True, True)
        for i, f in enumerate(range(3, -1, -1)):
            ub[1 + i, c] = _stage_block(upw[7 + f], 7 + f, c, True, False)
        db[0, c] = _stage_block(dww[7], 7, c, True, False)
        for i, f in enumerate(range(1, 5)):
            db[1 + i, c] = _stage_block(dww[7 + f], 7 + f, c, True, True)
        da[0, c] = _stage_block(dww[0], 0, c, False, False)
        for i, k in enumerate(range(1, 7)):
            da[1 + i, c] = _stage_block(dww[k], k, c, False, True)
    def tr(a):
        return np.ascontiguousarray(a.transpose(1, 2, 0, 3).reshape(32, 128, -1))
    return tr(ua), tr(ub), tr(da), tr(db)


def _host_consts():
    pos = np.arange(T, dtype=np.float64)
    m = np.arange(0, DH, 2, dtype=np.float64) / DH
    th = pos[:, None] / (10000.0 ** m[None, :])
    crow = np.repeat(np.cos(th).T, 2, axis=0)
    srow = np.repeat(np.sin(th).T, 2, axis=0)
    CS = np.zeros((2, 128, T), np.float32)
    CS[0, 0:64] = crow; CS[0, 64:128] = crow
    CS[1, 0:64] = srow; CS[1, 64:128] = srow
    MASK = np.zeros((4, 128, 512), np.float32)
    r = np.arange(128)[:, None]
    cc = np.arange(512)[None, :]
    for p in range(4):
        MASK[p] = np.where(128 * p + r > cc, -1e9, 0.0)
    AUX = np.zeros((128, 512), np.float32)
    for j in range(32):
        AUX[2 * j + 1, 2 * j] = -1.0
        AUX[2 * j, 2 * j + 1] = 1.0
        AUX[64 + 2 * j + 1, 64 + 2 * j] = -1.0
        AUX[64 + 2 * j, 64 + 2 * j + 1] = 1.0
    AUX[r[:, 0], 128 + r[:, 0]] = 1.0
    for k in range(64):
        AUX[k, 256 + 64 + k] = 1.0
    AUX[:, 384:512] = 1.0
    CONSTB = np.zeros((128, 8), np.float32)
    CONSTB[:, 0] = EXP_SHIFT
    CONSTB[:, 1] = 1.0 / D
    CONSTB[:, 2] = 1e-5
    CONSTB[:, 3] = -1.0
    return CS, MASK, AUX, CONSTB


ENGS = ("TEN", "ACT", "DVE", "POOL", "SP")


DMA_DEPTH = 4
QSEMS = [f"qs{i}" for i in range(DMA_DEPTH)] + [f"qg{i}" for i in range(DMA_DEPTH)]
CSEMS = ["pe", "act", "dve", "pool"]


class Builder:

    def __init__(self, nc, prof=False):
        self.nc = nc
        self.prof = prof
        self.scope = "init"
        self.ops = {e: [] for e in ENGS}
        self.cnt = {s: 0 for s in CSEMS + QSEMS}
        self.qgrp = {"s": 0, "g": 0}

    def emit(self, eng, fn, waits=(), inc=None, n=1):
        flat = []
        for t in waits:
            if t is None:
                continue
            for pair in t:
                flat.append(pair)
        self.ops[eng].append((flat, fn, inc, n, self.scope))
        if inc is not None:
            self.cnt[inc] += n
        return ((inc, self.cnt[inc]),) if inc else None

    def mm(self, out, lhsT, rhs, start, stop, waits=()):
        return self.emit("TEN",
                         lambda e, o=out, l=lhsT, r=rhs, a=start, b=stop:
                         e.matmul(o, l, r, start=a, stop=b), waits, "pe")

    def act(self, out, in_, func, waits=(), **kw):
        return self.emit("ACT",
                         lambda e, o=out, i=in_, f=func, k=dict(kw):
                         e.activation(o, i, f, **k), waits, "act")

    def dve(self, fn, waits=()):
        return self.emit("DVE", fn, waits, "dve")

    def pool(self, fn, waits=()):
        return self.emit("POOL", fn, waits, "pool")

    def dma(self, q, out, in_, waits=(), transpose=False, slow=False):
        eng = {"s": "SP", "g": "POOL"}[q]
        sem = f"q{q}{self.qgrp[q] % DMA_DEPTH}"
        self.qgrp[q] += 1
        if transpose:
            fn = lambda e, o=out, i=in_: e.dma_start(out=o, in_=i, transpose=True)
        elif slow:
            fn = lambda e, o=out, i=in_: e.dma_start(out=o, in_=i,
                                                     allow_slow_non_contiguous=True)
        else:
            fn = lambda e, o=out, i=in_: e.dma_start(out=o, in_=i)
        drain = ((sem, self.cnt[sem]),)
        return self.emit(eng, fn, tuple(waits) + (drain,), sem, 16)

    def qtoks(self):
        return tuple((s, self.cnt[s]) for s in QSEMS)

    def realize(self, block, sems):
        def replay(engname, engobj):
            hwm = {}
            cur = [None]
            import contextlib as _cl
            stack = _cl.ExitStack()
            for waits, fn, inc, n, scope in self.ops[engname]:
                if self.prof and scope != cur[0]:
                    stack.close()
                    stack = _cl.ExitStack()
                    stack.enter_context(self.nc.named_scope(scope))
                    cur[0] = scope
                for s, v in waits:
                    if hwm.get(s, -1) < v:
                        engobj.wait_ge(sems[s], v)
                        hwm[s] = v
                inst = fn(engobj)
                if inc is not None:
                    inst.then_inc(sems[inc], n)
            stack.close()
        @block.tensor
        def _(e): replay("TEN", e)
        @block.scalar
        def _(e): replay("ACT", e)
        @block.vector
        def _(e): replay("DVE", e)
        @block.gpsimd
        def _(e): replay("POOL", e)
        @block.sync
        def _(e): replay("SP", e)


def last(toks):
    m = {}
    for t in toks:
        if t is None:
            continue
        for sem, v in t:
            if m.get(sem, -1) < v:
                m[sem] = v
    return tuple(m.items()) if m else None


def build_program(dbg=False, prof=False):
    nc = bass.Bass("TRN2", target_bir_lowering=False)
    dt = nc.dram_tensor
    xin = dt("xin", [NSEQ, T, D], F32, kind="ExternalInput").ap()
    wq = dt("wq", [D, D], F32, kind="ExternalInput").ap()
    wk = dt("wk", [D, D], F32, kind="ExternalInput").ap()
    wv = dt("wv", [D, D], F32, kind="ExternalInput").ap()
    wo = dt("wo", [D, D], F32, kind="ExternalInput").ap()
    astk = dt("astk", [D, 40], F32, kind="ExternalInput").ap()
    bstk = dt("bstk", [40, D], F32, kind="ExternalInput").ap()
    wexp = dt("wexp", [D, HID], F32, kind="ExternalInput").ap()
    wcon = dt("wcon", [HID, D], F32, kind="ExternalInput").ap()
    bexp = dt("bexp", [HID], F32, kind="ExternalInput").ap()
    upb = dt("upb", [HID], F32, kind="ExternalInput").ap()
    bcon = dt("bcon", [D], F32, kind="ExternalInput").ap()
    dwnb = dt("dwnb", [D], F32, kind="ExternalInput").ap()
    cs_c = dt("cs_c", [2, 128, T], BF16, kind="ExternalInput").ap()
    mask_c = dt("mask_c", [4, 128, 512], BF16, kind="ExternalInput").ap()
    aux_c = dt("aux_c", [128, 512], BF16, kind="ExternalInput").ap()
    cb_c = dt("cb_c", [128, 8], F32, kind="ExternalInput").ap()
    ch_ua = dt("ch_ua", [32, 128, 7 * 128], F32, kind="ExternalInput").ap()
    ch_ub = dt("ch_ub", [32, 128, 5 * 128], F32, kind="ExternalInput").ap()
    ch_da = dt("ch_da", [32, 128, 7 * 128], F32, kind="ExternalInput").ap()
    ch_db = dt("ch_db", [32, 128, 5 * 128], F32, kind="ExternalInput").ap()

    scrP = dt("scrP", [D, HID], BF16, kind="Internal").ap()
    scrP2 = dt("scrP2", [HID, D], BF16, kind="Internal").ap()
    scrP3 = dt("scrP3", [HID, D], BF16, kind="Internal").ap()
    scrW1 = dt("scrW1", [HID, D], BF16, kind="Internal").ap()
    scrW2 = dt("scrW2", [HID, D], BF16, kind="Internal").ap()
    scrT = dt("scrT", [T, D], BF16, kind="Internal").ap()
    scrS1 = dt("scrS1", [T, D], F32, kind="Internal").ap()
    scrS2 = dt("scrS2", [T, D], F32, kind="Internal").ap()
    scrV = dt("scrV", [HID], BF16, kind="Internal").ap()
    yout = dt("yout", [NSEQ, T, D], F32, kind="ExternalOutput").ap()

    B = Builder(nc, prof=prof)
    ctx = contextlib.ExitStack()
    E = ctx.enter_context
    NB = 94464
    NF = 5856
    ab_t = E(nc.sbuf_tensor("ab", [128, NB], BF16))
    af_t = E(nc.sbuf_tensor("af", [128, NF], F32))
    ps_t = E(nc.psum_tensor("ps", [128, 4096], F32))

    def bank(i, width=512, p0=0, p1=128):
        return ps_t[p0:p1, 512 * i:512 * i + width]

    o_aux = 0
    o_bs = 512
    o_otmp = 1536
    o_b2 = 2048
    PERS = 4096
    o_xf = PERS
    o_pt = o_xf
    o_drow = o_xf + 5120
    o_drow2 = o_xf + 7168
    o_mask = o_xf + 9216
    o_zt2 = o_xf + 11264
    o_stash = o_xf + 13312
    o_stash2 = o_xf + 15104
    o_bx = o_xf + 15616
    o_qb = PERS + 16384
    o_kb = PERS + 32768
    o_vb = PERS + 49152
    o_acat = PERS + 65792
    o_ust = o_acat
    o_vst = o_acat + 2048
    o_cs = o_acat + 4096
    o_as = o_acat + 8192
    o_tqv = o_acat + 8512
    o_w = PERS + 82176
    assert o_w + 8192 <= NB


    fo_cb = 0
    fo_den = 8
    fo_dinvb = 1032
    fo_x0 = 1544
    fo_s = 3592
    fo_tmp = 4616
    fo_sm = 5640; fo_sq = 5656; fo_rstd = 5672; fo_nmr = 5688
    fo_sm2 = 5704; fo_sq2 = 5720; fo_rstd2 = 5736; fo_nmr2 = 5752
    fo_c1 = 5768
    fo_b1 = 5776
    fo_bb = 5808
    assert 5840 <= NF

    AUXB = ab_t[:, o_aux:o_aux + 512]
    JT = AUXB[:, 0:128]
    I128 = AUXB[:, 128:256]
    SHIFT = ab_t[0:64, o_aux + 256:o_aux + 384]
    ONES1_64 = ab_t[64:65, o_aux + 384:o_aux + 448]
    ONESROW = ab_t[0:1, o_aux + 384:o_aux + 512]
    MASKB = lambda p: ab_t[:, o_mask + 512 * p:o_mask + 512 * p + 512]
    CB = lambda i: af_t[:, fo_cb + i:fo_cb + i + 1]
    CSC = lambda off, n: ab_t[:, o_cs + off:o_cs + off + n]
    XF = lambda c: ab_t[:, o_xf + 2048 * c:o_xf + 2048 * c + 2048]
    QB = lambda c: ab_t[:, o_qb + 2048 * c:o_qb + 2048 * c + 2048]
    KB = lambda c: ab_t[:, o_kb + 2048 * c:o_kb + 2048 * c + 2048]
    VE = lambda ts: ab_t[:, o_vb + 1040 * ts:o_vb + 1040 * ts + 1040]
    ACAT = lambda c: ab_t[:, o_acat + 2048 * c:o_acat + 2048 * c + 2048]
    WB = lambda c: ab_t[:, o_w + 1024 * c:o_w + 1024 * c + 1024]
    ASB = lambda c: ab_t[:, o_as + 40 * c:o_as + 40 * c + 40]
    PT = lambda kc: ab_t[:, o_pt + 512 * kc:o_pt + 512 * kc + 512]
    ZT = lambda c: ab_t[:, o_z + 1024 * c:o_z + 1024 * c + 1024]
    STASH = lambda i: ab_t[:, o_stash + 128 * i:o_stash + 128 * i + 128]
    RS = lambda i: ab_t[:, o_stash2 + 128 * i:o_stash2 + 128 * i + 128]
    BXC = lambda c: ab_t[:, o_bx + c:o_bx + c + 1]
    B2HI = lambda di: ab_t[0:1, o_b2 + 512 * di:o_b2 + 512 * di + 512]
    B2LO = lambda di: ab_t[0:1, o_b2 + 1024 + 512 * di:o_b2 + 1024 + 512 * di + 512]
    W1T = lambda dc: ab_t[:, o_qb + 4096 * dc:o_qb + 4096 * dc + 4096]
    W2T = lambda hc: ab_t[:, o_vb + 1024 * hc:o_vb + 1024 * hc + 1024]
    HACT = lambda i: ab_t[:, o_w + 256 * i:o_w + 256 * i + 256]
    SC5 = ab_t[:, o_zt2:o_zt2 + 1024]
    JUNK6 = ab_t[:, o_w + 1024:o_w + 2048]

    tk_aux = B.dma("s", AUXB, aux_c)
    tk_cb = B.dma("s", af_t[:, fo_cb:fo_cb + 8], cb_c)
    tk_bs = B.dma("g", ab_t[0:40, o_bs:o_bs + 1024], bstk)

    B.scope = 'prep'

    def pi_view(t, cs=None):
        v = t.rearrange("(a b c) f -> c b a f", a=32, b=4, c=32)
        return v if cs is None else v[cs]

    ZT2 = lambda i: ab_t[:, o_zt2 + 1024 * i:o_zt2 + 1024 * i + 1024]
    STASH = lambda q, i: ab_t[:, o_stash + 896 * q + 128 * i:
                              o_stash + 896 * q + 128 * i + 128]
    RS = lambda i: ab_t[:, o_stash2 + 128 * i:o_stash2 + 128 * i + 128]
    BXC = lambda c: ab_t[:, o_bx + c:o_bx + c + 1]

    tk_bc = B.dma("s", af_t[0:1, fo_s:fo_s + 1024], bcon.rearrange("(o d) -> o d", o=1))
    tk_db_ = B.dma("s", af_t[0:1, fo_tmp:fo_tmp + 1024], dwnb.rearrange("(o d) -> o d", o=1))
    tk_b2 = B.dve(lambda e: e.tensor_tensor(out=af_t[0:1, fo_s:fo_s + 1024],
                                            in0=af_t[0:1, fo_s:fo_s + 1024],
                                            in1=af_t[0:1, fo_tmp:fo_tmp + 1024],
                                            op=ALU.add), waits=(tk_bc, tk_db_))
    tk_b2h = B.dve(lambda e: e.tensor_copy(out=ab_t[0:1, o_b2:o_b2 + 1024],
                                           in_=af_t[0:1, fo_s:fo_s + 1024]), waits=(tk_b2,))
    tk_b2l = B.dve(lambda e: e.tensor_tensor(out=ab_t[0:1, o_b2 + 1024:o_b2 + 2048],
                                             in0=af_t[0:1, fo_s:fo_s + 1024],
                                             in1=ab_t[0:1, o_b2:o_b2 + 1024],
                                             op=ALU.subtract), waits=(tk_b2h,))

    tk_stage = [B.dma("g", ab_t[:, o_xf + 4096 * rg:o_xf + 4096 * rg + 4096],
                      wexp[128 * rg:128 * rg + 128, :]) for rg in range(8)]
    tk_wstage = [B.dma("s", scrP[128 * rg:128 * rg + 128, :],
                       ab_t[:, o_xf + 4096 * rg:o_xf + 4096 * rg + 4096],
                       waits=(tk_stage[rg],)) for rg in range(8)]
    tok_wstage = last(tk_wstage)
    tk_upb = B.dma("s", af_t[:, fo_bb:fo_bb + 32],
                   upb.rearrange("(a b c) -> b a c", a=32, b=4, c=32))

    prep_fin = {"w1": None, "w2": None, "dve": None, "b1": None}

    def _prep_gen(bank_gate):
        war = {0: bank_gate}
        stash_tok = {0: bank_gate, 1: bank_gate}
        zt_tok = {0: bank_gate, 1: bank_gate}
        chunk_res = [None]
        tk_bx = B.dma("g", ab_t[:, o_bx:o_bx + 32],
                      bexp.rearrange("(c p) -> p c", p=128), slow=True,
                      waits=(bank_gate,))

        def chunk(ch_dram, nmat, cs, load_fn, store_fn, bcol, tok_b):
            q = cs % 2
            bk_ = 2
            tkz = load_fn(ZT2(q), (zt_tok[q],))
            tk_st = B.dma("g", ab_t[:, o_stash + 896 * q:o_stash + 896 * q + 128 * nmat],
                          ch_dram[cs], waits=(stash_tok[q],))
            yield
            cur = STASH(q, 0)
            tk_prev = tk_st
            for i in range(1, nmat):
                tkm = B.mm(bank(bk_, 128), STASH(q, i), cur, True, True,
                           waits=(tk_st, tk_prev, war[0]))
                dstr = RS(2 * q + (i % 2))
                tk_prev = B.dve(lambda e, d=dstr, s_=bank(bk_, 128):
                                e.tensor_copy(out=d, in_=s_), waits=(tkm,))
                cur = dstr
                yield
            t1 = B.mm(bank(bk_), cur, ZT2(q)[:, 0:512], True, True,
                      waits=(tkz, tk_prev, war[0]))
            e1 = B.dve(lambda e, d=ZT2(q)[:, 0:512], s_=bank(bk_):
                       e.tensor_copy(out=d, in_=s_), waits=(t1,))
            yield
            t2 = B.mm(bank(bk_), cur, ZT2(q)[:, 512:1024], True, True, waits=(e1,))
            e2 = B.dve(lambda e, d=ZT2(q)[:, 512:1024], s_=bank(bk_):
                       e.tensor_copy(out=d, in_=s_), waits=(t2,))
            yield
            tb = None
            last_pe = t2
            if bcol is not None:
                t3 = B.mm(bank(bk_, 1), cur, bcol, True, True, waits=(tok_b, e2))
                tb = B.dve(lambda e, d=bcol, s_=bank(bk_, 1):
                           e.tensor_copy(out=d, in_=s_), waits=(t3,))
                last_pe = t3
                yield
            war[0] = last([e1, e2, tb])
            stash_tok[q] = last_pe
            tks = store_fn(ZT2(q), (last([e1, e2]),))
            zt_tok[q] = tks
            chunk_res[0] = (tks, last([e1, e2, tb]))

        ua_st = []
        ua_b = []
        for hc in range(32):
            yield from chunk(
                ch_ua, 7, hc,
                lambda d, w, hc=hc: B.dma("s", d, scrP[0:1024, 128 * hc:128 * hc + 128],
                                          waits=w + (tok_wstage,), transpose=True),
                lambda t, w, hc=hc: B.dma("s", scrP2[128 * hc:128 * hc + 128, :], t, waits=w),
                BXC(hc), tk_bx)
            tks, tball = chunk_res[0]
            ua_st.append(tks)
            ua_b.append(tball)
            yield
        tok_ua = last(ua_st)
        tkv1 = B.dma("s", scrV.rearrange("(c p) -> p c", p=128),
                     ab_t[:, o_bx:o_bx + 32], waits=(tok_ua, last(ua_b)), slow=True)
        tkv2 = B.dma("s", ab_t[:, o_bx:o_bx + 32],
                     scrV.rearrange("(a b c) -> b a c", a=32, b=4, c=32), waits=(tkv1,))
        vb_st = []
        vb_b = []
        for cs in range(32):
            yield from chunk(
                ch_ub, 5, cs,
                lambda d, w, cs=cs: B.dma("s", d, pi_view(scrP2, cs), waits=w + (tok_ua,)),
                lambda t, w, cs=cs: B.dma("s", scrW1[128 * cs:128 * cs + 128, :], t, waits=w),
                BXC(cs), tkv2)
            tks, tball = chunk_res[0]
            vb_st.append(tks)
            vb_b.append(tball)
            yield
        prep_fin["w1"] = last(vb_st)
        prep_fin["b1"] = B.dve(
            lambda e: e.tensor_tensor(out=af_t[:, fo_b1:fo_b1 + 32],
                                      in0=af_t[:, fo_bb:fo_bb + 32],
                                      in1=ab_t[:, o_bx:o_bx + 32], op=ALU.add),
            waits=(last(vb_b), tk_upb))
        w2m_st = []
        for cs in range(32):
            yield from chunk(
                ch_db, 5, cs,
                lambda d, w, cs=cs: B.dma("g", d, pi_view(wcon, cs), waits=w),
                lambda t, w, cs=cs: B.dma("s", scrP3[128 * cs:128 * cs + 128, :], t, waits=w),
                None, None)
            w2m_st.append(chunk_res[0][0])
            yield
        tok_w2m = last(w2m_st)
        w2_st = []
        for hc in range(32):
            yield from chunk(
                ch_da, 7, hc,
                lambda d, w, hc=hc: B.dma("s", d, pi_view(scrP3, hc), waits=w + (tok_w2m,)),
                lambda t, w, hc=hc: B.dma("s", pi_view(scrW2, hc), t, waits=w),
                None, None)
            w2_st.append(chunk_res[0][0])
            yield
        prep_fin["w2"] = last(w2_st)
        while True:
            yield

    B1C = lambda hc: af_t[:, fo_b1 + hc:fo_b1 + hc + 1]
    B2HI = lambda di: ab_t[0:1, o_b2 + 512 * di:o_b2 + 512 * di + 512]
    B2LO = lambda di: ab_t[0:1, o_b2 + 1024 + 512 * di:o_b2 + 1024 + 512 * di + 512]

    ones_dest = ab_t[:, o_vb:o_vb + 16640].rearrange(
        "p (t g s) -> p t g s", t=16, g=16, s=65)[:, :, :, 64:65]

    y_toks = []
    prev_den = {0: None, 1: None}
    prev_seq_gate = (tk_bs,)

    def emit_p2(sq, gates):
        B.scope = f'p2_{sq}'
        tk_x = B.dma("g", ab_t[:, o_qb:o_qb + 16384],
                     xin[sq].rearrange("(t p) f -> p t f", p=128), waits=gates)
        tk_xs = B.dma("s", scrT.rearrange("(t p) f -> p t f", p=128),
                      ab_t[:, o_qb:o_qb + 16384], waits=(tk_x,) + gates)
        return last([B.dma("s", XF(dc), scrT[:, 128 * dc:128 * dc + 128],
                           waits=(tk_xs,), transpose=True) for dc in range(8)])

    p2tok = {0: emit_p2(0, (tok_wstage, tk_bs))}
    p2gate = None
    p3_gate = (tk_bs,)
    w_gate = (tk_bs,)
    bank_gate = None
    p7gen = None
    p7state = {}
    for sq in range(NSEQ):
        tk_vones = B.pool(lambda e, d=ones_dest: e.memset(d, 1.0),
                          waits=p3_gate)
        if sq not in p2tok:
            p2tok[sq] = emit_p2(sq, p2gate)
        tok_xf = p2tok[sq]

        B.scope = f'p3_{sq}'
        tk_cs = B.dma("s", CSC(0, 2 * T), cs_c.rearrange("n p f -> p n f"),
                      waits=p3_gate)
        tk_as = B.dma("g", ab_t[:, o_as:o_as + 320],
                      astk.rearrange("(c p) r -> p c r", p=128), waits=p3_gate)
        tok_tqv = []
        for tt in range(4):
            tprev = None
            for c in range(8):
                tprev = B.mm(bank(6, 512, 0, 40), ASB(c), XF(c)[:, 512 * tt:512 * tt + 512],
                             c == 0, c == 7,
                             waits=(tok_xf, tk_as, tk_bs, bank_gate,
                                    tok_tqv[tt - 1] if tt else None) if c == 0 else ())
            tok_tqv.append(B.act(ab_t[0:40, o_tqv + 512 * tt:o_tqv + 512 * tt + 512],
                                 bank(6, 512, 0, 40), AF.Identity, scale=4.0,
                                 waits=(tprev,)))

        tk_wv = B.dma("g", ab_t[:, o_w:o_w + 8192],
                      wv.rearrange("(c p) f -> p c f", p=128), waits=w_gate)
        last_evac = {}
        for u, (ts, fo) in enumerate([(a, b) for a in range(16) for b in range(2)]):
            bk = u % 4
            tprev = None
            for c in range(8):
                tprev = B.mm(bank(bk), XF(c)[:, 128 * ts:128 * ts + 128],
                             WB(c)[:, 512 * fo:512 * fo + 512], c == 0, False,
                             waits=(tk_wv, tok_tqv[ts // 4], last_evac.get(bk))
                             if c == 0 else ())
            tprev = B.mm(bank(bk), ab_t[32:40, o_tqv + 128 * ts:o_tqv + 128 * ts + 128],
                         ab_t[32:40, o_bs + 512 * fo:o_bs + 512 * fo + 512],
                         False, True, waits=())
            vdst = VE(ts).rearrange("p (g s) -> p g s", g=16, s=65)[:, 8 * fo:8 * fo + 8, 0:64]
            last_evac[bk] = B.dve(lambda e, d=vdst, s=bank(bk): e.tensor_copy(out=d, in_=s),
                                  waits=(tprev,))
        tok_v = last(list(last_evac.values()))

        def proj_rope(wdram, dst, lora, wwait, drive=None):
            tk_w = B.dma("g", ab_t[:, o_w:o_w + 8192],
                         wdram.rearrange("(c p) f -> p c f", p=128), waits=(wwait,))
            lastu, lastj, lastq, lastjb = {}, {}, {}, {}
            toks = []
            for u, (co, tt) in enumerate([(a, b) for a in range(8) for b in range(4)]):
                bk = u % 4
                jb = 4 + (u % 4)
                tprev = None
                for c in range(8):
                    tprev = B.mm(bank(bk), WB(c)[:, 128 * co:128 * co + 128],
                                 XF(c)[:, 512 * tt:512 * tt + 512], c == 0,
                                 (c == 7) and not lora,
                                 waits=(tk_w, lastu.get(bk),
                                        tok_tqv[tt] if lora else None) if c == 0 else ())
                if lora:
                    tprev = B.mm(bank(bk), ab_t[0:8, o_bs + 128 * co:o_bs + 128 * co + 128],
                                 ab_t[0:8, o_tqv + 512 * tt:o_tqv + 512 * tt + 512],
                                 False, True, waits=())
                sl = u % 4
                us = ab_t[:, o_ust + 512 * sl:o_ust + 512 * sl + 512]
                vs = ab_t[:, o_vst + 512 * sl:o_vst + 512 * sl + 512]
                tu = B.dve(lambda e, d=us, s=bank(bk), cs_=CSC(T + 512 * tt, 512):
                           e.tensor_tensor(out=d, in0=s, in1=cs_, op=ALU.mult),
                           waits=(tprev, lastj.get(sl), tk_cs))
                tj = B.mm(bank(jb), JT, us, True, True,
                          waits=(tu, tk_aux, lastjb.get(jb, bank_gate)))
                tv = B.dve(lambda e, d=vs, s=bank(bk), cs_=CSC(512 * tt, 512):
                           e.tensor_tensor(out=d, in0=s, in1=cs_, op=ALU.mult),
                           waits=(tprev, lastj.get(sl)))
                lastu[bk] = tv
                dd = dst(co)[:, 512 * tt:512 * tt + 512]
                tr = B.dve(lambda e, d=dd, a=vs, b=bank(jb):
                           e.tensor_tensor(out=d, in0=b, in1=a, op=ALU.add),
                           waits=(tv, tj))
                lastj[sl] = tr
                lastjb[jb] = tr
                toks.append(tr)
                if drive:
                    drive()
            return last(toks)

        drvn = [0]

        def drv():
            drvn[0] += 1
            if p7gen is not None and drvn[0] % 2 == 0:
                next(p7gen, None)

        tok_q = proj_rope(wq, QB, True, tok_v, drive=drv)
        tok_k = proj_rope(wk, KB, False, tok_q, drive=drv)
        if p7gen is not None:
            for _ in p7gen:
                pass
            p7gen = None
        if sq > 0:
            prev_seq_gate = p7state['gate']
            prev_den = {h: last([prev_den[h], prev_seq_gate[0]]) for h in (0, 1)}

        B.scope = f'p4_{sq}'
        tk_mask = B.dma("s", ab_t[:, o_mask:o_mask + 2048],
                        mask_c.rearrange("n p f -> p n f"), waits=(tok_k,))
        if sq == 0:
            prep_gen = _prep_gen(tok_k)
        tok_acat = []
        last_sbank = {}
        last_ptg = {}
        last_pv = {}
        last_dbank = {}
        last_dinvb = {}
        last_otmp = [None]
        prev_den2 = {(h, p): prev_den[h] for h in (0, 1) for p in (0, 1)}
        pv_tok = {}
        pend = []
        dgens = []

        def drain_dg():
            while dgens:
                try:
                    next(dgens[0])
                except StopIteration:
                    dgens.pop(0)

        def drive_dg(n):
            done = 0
            while dgens and done < n:
                try:
                    next(dgens[0])
                    done += 1
                except StopIteration:
                    dgens.pop(0)

        def emit_pv(s, hp, j, kc, te0, te1):
            if kc == 0:
                drain_dg()
            for half, te in ((0, te0), (1, te1)):
                pb = 4 + 2 * half + (j % 2)
                pv = B.mm(bank(pb, 512, 0, 65),
                          VE(kc)[:, 65 * (2 * hp + half):65 * (2 * hp + half) + 65],
                          ab_t[:, o_pt + 1024 * s + 512 * half:
                               o_pt + 1024 * s + 512 * half + 512],
                          kc == 0, kc == 4 * j + 3,
                          waits=(te, last_pv.get((half, j % 2)), tok_v, tk_vones)
                          if kc == 0 else (te,))
                if kc == 4 * j + 3:
                    pv_tok[(hp, half, j)] = pv
                last_ptg[(s, half)] = pv

        def den_gen(hp, j):
            while ((hp, 0, j) not in pv_tok) or ((hp, 1, j) not in pv_tok):
                yield
            jj = j % 2
            for half in (0, 1):
                pbase = 2048 + 1024 * half + 512 * jj
                fo_dh = (fo_den if half == 0 else fo_x0) + 512 * jj
                o_drh = o_drow if half == 0 else o_drow2
                tln = B.act(af_t[64:65, fo_dh:fo_dh + 512],
                            ps_t[64:65, pbase:pbase + 512],
                            AF.Ln, waits=(pv_tok[(hp, half, j)],
                                          prev_den2[(half, jj)]))
                yield
                tex = B.act(af_t[64:65, fo_dh:fo_dh + 512],
                            af_t[64:65, fo_dh:fo_dh + 512],
                            AF.Exp, scale=-1.0, waits=(tln,))
                yield
                thi = B.dve(lambda e, o=ab_t[64:65, o_drh + 512 * jj:
                                             o_drh + 512 * jj + 512],
                            i=af_t[64:65, fo_dh:fo_dh + 512]:
                            e.tensor_copy(out=o, in_=i),
                            waits=(tex, prev_den2[(half, jj)]))
                yield
                tlo = B.dve(lambda e, o=ab_t[64:65, o_drh + 1024 + 512 * jj:
                                             o_drh + 1024 + 512 * jj + 512],
                            a=af_t[64:65, fo_dh:fo_dh + 512],
                            b=ab_t[64:65, o_drh + 512 * jj:
                                   o_drh + 512 * jj + 512]:
                            e.tensor_tensor(out=o, in0=a, in1=b,
                                            op=ALU.subtract), waits=(thi,))
                yield
                db_ = 3
                pb = 4 + 2 * half + jj
                tb1 = B.mm(bank(db_, 512, 0, 64), ONES1_64,
                           ab_t[64:65, o_drh + 512 * jj:
                                o_drh + 512 * jj + 512],
                           True, False,
                           waits=(thi, last_dbank.get(db_)))
                tb2 = B.mm(bank(db_, 512, 0, 64), ONES1_64,
                           ab_t[64:65, o_drh + 1024 + 512 * jj:
                                o_drh + 1024 + 512 * jj + 512],
                           False, True, waits=(tlo,))
                yield
                tev = B.act(af_t[0:64, fo_dinvb + 512 * jj:
                                 fo_dinvb + 512 * jj + 512],
                            bank(db_, 512, 0, 64),
                            AF.Identity, scale=1.0,
                            waits=(tb2, last_dinvb.get(jj)))
                last_dbank[db_] = tev
                yield
                dinvb = af_t[0:64, fo_dinvb + 512 * jj:
                             fo_dinvb + 512 * jj + 512]
                if half == 0:
                    dst = ACAT(hp)[0:64, 512 * j:512 * j + 512]
                    tn = B.dve(lambda e, d=dst, s=bank(pb, 512, 0, 64),
                               v=dinvb:
                               e.tensor_tensor(out=d, in0=s, in1=v,
                                               op=ALU.mult), waits=(tev,))
                    last_dinvb[jj] = tn
                    last_pv[(half, jj)] = tn
                    prev_den2[(half, jj)] = tn
                    prev_den[half] = tn
                    tok_acat.append(tn)
                    yield
                else:
                    tn = B.dve(lambda e, s=bank(pb, 512, 0, 64), v=dinvb:
                               e.tensor_tensor(
                                   out=ab_t[0:64, o_otmp:o_otmp + 512],
                                   in0=s, in1=v, op=ALU.mult),
                               waits=(tev, last_otmp[0]))
                    yield
                    tsh = B.mm(bank(db_), SHIFT,
                               ab_t[0:64, o_otmp:o_otmp + 512],
                               True, True, waits=(tn,))
                    last_otmp[0] = tsh
                    yield
                    dst = ACAT(hp)[64:128, 512 * j:512 * j + 512]
                    tn2 = B.dve(lambda e, d=dst, s=bank(db_, 512, 64, 128):
                                e.tensor_copy(out=d, in_=s), waits=(tsh,))
                    last_dbank[db_] = tn2
                    last_dinvb[jj] = tn2
                    last_pv[(half, jj)] = tn2
                    prev_den2[(half, jj)] = tn2
                    prev_den[half] = tn2
                    tok_acat.append(tn2)
                    yield

        it_ctr = [0]
        for hp in range(8):
            c = hp
            for j in range(4):
                for kc in range(4 * j + 4):
                    s = it_ctr[0] % 5
                    it_ctr[0] += 1
                    diag = kc >= 4 * j
                    mm0 = B.mm(bank(0),
                               KB(c)[0:64, 128 * kc:128 * kc + 128],
                               QB(c)[0:64, 512 * j:512 * j + 512],
                               True, not diag,
                               waits=(last_sbank.get(0), tok_k, tk_mask))
                    if diag:
                        mm0 = B.mm(bank(0), I128, MASKB(kc % 4),
                                   False, True, waits=())
                    te0 = B.act(ab_t[:, o_pt + 1024 * s:o_pt + 1024 * s + 512],
                                ps_t[:, 0:512], AF.Exp, scale=0.125, bias=CB(0),
                                waits=(mm0, last_ptg.get((s, 0))))
                    mm1 = B.mm(bank(1),
                               KB(c)[64:128, 128 * kc:128 * kc + 128],
                               QB(c)[64:128, 512 * j:512 * j + 512],
                               True, not diag,
                               waits=(last_sbank.get(1), tok_k, tk_mask))
                    if diag:
                        mm1 = B.mm(bank(1), I128, MASKB(kc % 4),
                                   False, True, waits=())
                    te1 = B.act(ab_t[:, o_pt + 1024 * s + 512:o_pt + 1024 * s + 1024],
                                ps_t[:, 512:1024], AF.Exp, scale=0.125, bias=CB(0),
                                waits=(mm1, last_ptg.get((s, 1))))
                    last_sbank[0] = te0
                    last_sbank[1] = te1
                    pend.append((s, hp, j, kc, te0, te1))
                    if len(pend) > 3:
                        emit_pv(*pend.pop(0))
                    if sq == 0:
                        for _ in range(4):
                            next(prep_gen)
                    drive_dg(3)
                while len(pend) > 2:
                    emit_pv(*pend.pop(0))
                dgens.append(den_gen(hp, j))
        while pend:
            emit_pv(*pend.pop(0))
        drain_dg()
        if sq == 0:
            for _ in range(400):
                next(prep_gen)
        tok_acat = last(tok_acat)

        B.scope = f'p5_{sq}'
        tk_wo = B.dma("g", ab_t[:, o_w:o_w + 8192],
                      wo.rearrange("(c p) f -> p c f", p=128), waits=(tok_k,))
        S1T = af_t[:, fo_s:fo_s + 1024]
        tok_x1 = []
        x1_half = []
        wo_mms = []
        tk_x0 = {}
        ttdone = {}
        last_s1bank = {}
        prev_gate = None
        prev_ln = None
        tk_x0[0] = B.dma("s", af_t[:, fo_x0:fo_x0 + 1024], xin[sq, 0:128, :],
                         waits=prev_seq_gate + (prev_den[0], prev_den[1], tok_acat))
        tk_x0[1] = B.dma("s", af_t[:, fo_x0 + 1024:fo_x0 + 2048], xin[sq, 128:256, :],
                         waits=prev_seq_gate + (prev_den[0], prev_den[1], tok_acat))
        tok_w1 = last([B.dma("s", W1T(dc), scrW1[:, 128 * dc:128 * dc + 128],
                             waits=(tok_acat, prep_fin["w1"], prep_fin["b1"]),
                             transpose=True)
                       for dc in range(8)])
        tok_xfb = {}
        tk_w2p = []
        w2view = scrW2.rearrange("(c p) f -> p c f", p=128)
        for ts in range(16):
            if ts == 8:
                tok_xfb[0] = last([B.dma("s", XF(dc)[:, 0:1024],
                                         scrT[0:1024, 128 * dc:128 * dc + 128],
                                         waits=(last(x1_half), prep_fin["w2"]),
                                         transpose=True)
                                   for dc in range(8)])
            B.scope = f'p5mm_{sq}'
            tts = []
            for fo in range(2):
                bk = 2 * (ts % 2) + fo
                tprev = None
                for c in range(8):
                    tprev = B.mm(bank(bk), ACAT(c)[:, 128 * ts:128 * ts + 128],
                                 WB(c)[:, 512 * fo:512 * fo + 512], c == 0, c == 7,
                                 waits=(tok_acat, tk_wo, last_s1bank.get(bk),
                                        prep_fin["w1"], prep_fin["w2"])
                                 if c == 0 else ())
                if ts == 15:
                    wo_mms.append(tprev)
                x0s = af_t[:, fo_x0 + 1024 * (ts % 2) + 512 * fo:
                           fo_x0 + 1024 * (ts % 2) + 512 * fo + 512]
                tt = B.dve(lambda e, d=S1T[:, 512 * fo:512 * fo + 512], s=bank(bk), x=x0s:
                           e.tensor_tensor(out=d, in0=s, in1=x, op=ALU.add),
                           waits=(tprev, tk_x0[ts], prev_gate))
                last_s1bank[bk] = tt
                tts.append(tt)
            ttdone[ts] = last(tts)
            if ts == 15:
                for i in range(4):
                    tk_w2p.append(B.dma("sg"[i % 2],
                                        ab_t[:, o_vb + 8192 * i:o_vb + 8192 * i + 8192],
                                        w2view[:, 8 * i:8 * i + 8, :],
                                        waits=(wo_last := last(wo_mms), tok_acat,
                                               prep_fin["w2"])))
            if ts + 2 < 16:
                tk_x0[ts + 2] = B.dma("s",
                                      af_t[:, fo_x0 + 1024 * (ts % 2):fo_x0 + 1024 * (ts % 2) + 1024],
                                      xin[sq, 128 * (ts + 2):128 * (ts + 2) + 128, :],
                                      waits=(ttdone[ts],))
            B.scope = f'p5st_{sq}'
            tsq = B.act(SC5, S1T, AF.Square,
                        accum_out=af_t[:, fo_sq + ts:fo_sq + ts + 1],
                        waits=tuple(tts) + ((prep_fin["w2"],) if ts == 0 else ()))
            tsm = B.dve(lambda e, o=af_t[:, fo_sm + ts:fo_sm + ts + 1]:
                        e.tensor_reduce(out=o, in_=S1T, axis=AX.X, op=ALU.add),
                        waits=tuple(tts))
            B.scope = f'p5ln_{sq}'
            c1 = af_t[:, fo_c1:fo_c1 + 1]
            c2 = af_t[:, fo_c1 + 1:fo_c1 + 2]
            c3 = af_t[:, fo_c1 + 4:fo_c1 + 5]
            t1 = B.dve(lambda e, o=c1, i=af_t[:, fo_sm + ts:fo_sm + ts + 1]:
                       e.tensor_tensor(out=o, in0=i, in1=CB(1), op=ALU.mult),
                       waits=(tsm, prev_ln))
            t2 = B.dve(lambda e, o=c2, i=af_t[:, fo_sq + ts:fo_sq + ts + 1]:
                       e.tensor_tensor(out=o, in0=i, in1=CB(1), op=ALU.mult),
                       waits=(tsq, prev_ln))
            t3a = B.dve(lambda e, o=c3, b=c1:
                        e.tensor_tensor(out=o, in0=b, in1=b, op=ALU.mult), waits=(t1,))
            t3 = B.dve(lambda e, o=c2, a=c2, b=c3:
                       e.tensor_tensor(out=o, in0=a, in1=b, op=ALU.subtract),
                       waits=(t2, t3a))
            t4 = B.dve(lambda e, o=c2, a=c2:
                       e.tensor_tensor(out=o, in0=a, in1=CB(2), op=ALU.add), waits=(t3,))
            t5 = B.act(c2, c2, AF.Ln, waits=(t4,))
            t6 = B.act(af_t[:, fo_rstd + ts:fo_rstd + ts + 1], c2, AF.Exp, scale=-0.5,
                       waits=(t5,))
            t7 = B.dve(lambda e, o=af_t[:, fo_nmr + ts:fo_nmr + ts + 1], a=c1,
                       r=af_t[:, fo_rstd + ts:fo_rstd + ts + 1]:
                       e.scalar_tensor_tensor(out=o, in0=a, scalar=CB(3), in1=r,
                                              op0=ALU.mult, op1=ALU.mult), waits=(t6,))
            prev_ln = last([t7, t6])
            B.scope = f'p5sp_{sq}'
            tx1 = B.act(SC5, S1T, AF.Identity,
                        scale=af_t[:, fo_rstd + ts:fo_rstd + ts + 1],
                        bias=af_t[:, fo_nmr + ts:fo_nmr + ts + 1], waits=(t7,))
            tk_sp = B.dma("s", scrS1[128 * ts:128 * ts + 128, :], S1T, waits=(tsq, tsm))
            tk_x1 = B.dma("s", scrT[128 * ts:128 * ts + 128, :], SC5, waits=(tx1,))
            prev_gate = last([tk_x1, tk_sp])
            tok_x1 += [tk_x1, tk_sp]
            if ts <= 7:
                x1_half.append(tk_x1)
        tok_x1 = last(tok_x1)
        tok_x1h = last(x1_half)
        wo_last = last(wo_mms)
        s1bank_tok = dict(last_s1bank)

        B.scope = f'p6_{sq}'
        tok_xfb[1] = last([B.dma("s", XF(dc)[:, 1024:2048],
                                 scrT[1024:2048, 128 * dc:128 * dc + 128],
                                 waits=(tok_x1,), transpose=True)
                           for dc in range(8)])
        tk_w2l = last(tk_w2p)
        tk_s1r = {}
        sttdone = {}
        seq_stats = {}
        prev_s2sp = None
        last_gelu = {}
        last_w2use = {}
        last_obank = {}
        tk_s1r[0] = B.dma("s", af_t[:, fo_x0:fo_x0 + 1024], scrS1[0:128, :],
                          waits=(tok_x1,))
        tk_s1r[1] = B.dma("s", af_t[:, fo_x0 + 1024:fo_x0 + 2048], scrS1[128:256, :],
                          waits=(tok_x1,))
        for Bi in range(8):
            gelu_t = {}
            def w1_step(hc, Bi=Bi):
                pb = hc % 4
                tprev = None
                for cix in range(8):
                    tprev = B.mm(ps_t[:, 512 * pb:512 * pb + 256],
                                 W1T(cix)[:, 128 * hc:128 * hc + 128],
                                 XF(cix)[:, 256 * Bi:256 * Bi + 256], cix == 0, cix == 7,
                                 waits=(tok_w1, tok_xfb[Bi // 4],
                                        last_gelu.get(pb, s1bank_tok.get(pb)))
                                 if cix == 0 else ())
                tg = B.act(HACT(pb), ps_t[:, 512 * pb:512 * pb + 256], AF.Gelu,
                           bias=B1C(hc), waits=(tprev, last_w2use.get(pb)))
                last_gelu[pb] = tg
                gelu_t[hc] = tg
            def w2_step(hc):
                pb = hc % 4
                tg = gelu_t[hc]
                wprev = None
                for ti in range(2):
                    for di in range(2):
                        ob = 4 + 2 * ti + di
                        wprev = B.mm(bank(ob), HACT(pb)[:, 128 * ti:128 * ti + 128],
                                     W2T(hc)[:, 512 * di:512 * di + 512], hc == 0, False,
                                     waits=(tg, tk_w2l, last_obank.get(ob))
                                     if hc == 0 else (tg,))
                last_w2use[pb] = wprev
            w1_step(0)
            w1_step(1)
            for hc in range(2, 32):
                w1_step(hc)
                w2_step(hc - 2)
            w2_step(30)
            w2_step(31)
            for ti in range(2):
                ts = 2 * Bi + ti
                obank_toks = []
                for di in range(2):
                    ob = 4 + 2 * ti + di
                    B.mm(bank(ob), ONESROW, B2HI(di), False, False,
                         waits=(tk_b2l,))
                    tb = B.mm(bank(ob), ONESROW, B2LO(di), False, True, waits=())
                    s1s = af_t[:, fo_x0 + 1024 * (ts % 2) + 512 * di:
                               fo_x0 + 1024 * (ts % 2) + 512 * di + 512]
                    tt = B.dve(lambda e, o=af_t[:, fo_tmp + 512 * di:fo_tmp + 512 * di + 512],
                               a=s1s, r=af_t[:, fo_rstd + ts:fo_rstd + ts + 1], p=bank(ob):
                               e.scalar_tensor_tensor(out=o, in0=a, scalar=r, in1=p,
                                                      op0=ALU.mult, op1=ALU.add),
                               waits=(tb, tk_s1r[ts]))
                    last_obank[ob] = tt
                    obank_toks.append(tt)
                sttdone[ts] = last(obank_toks)
                if ts + 2 < 16:
                    tk_s1r[ts + 2] = B.dma("s",
                                           af_t[:, fo_x0 + 1024 * (ts % 2):fo_x0 + 1024 * (ts % 2) + 1024],
                                           scrS1[128 * (ts + 2):128 * (ts + 2) + 128, :],
                                           waits=(sttdone[ts],))
                ts2 = B.act(af_t[:, fo_s:fo_s + 1024], af_t[:, fo_tmp:fo_tmp + 1024],
                            AF.Identity, scale=1.0,
                            bias=af_t[:, fo_nmr + ts:fo_nmr + ts + 1],
                            waits=(sttdone[ts], prev_s2sp))
                tsq2 = B.act(JUNK6, af_t[:, fo_s:fo_s + 1024], AF.Square,
                             accum_out=af_t[:, fo_sq2 + ts:fo_sq2 + ts + 1], waits=(ts2,))
                tsm2 = B.dve(lambda e, o=af_t[:, fo_sm2 + ts:fo_sm2 + ts + 1]:
                             e.tensor_reduce(out=o, in_=af_t[:, fo_s:fo_s + 1024],
                                             axis=AX.X, op=ALU.add), waits=(ts2,))
                tk_s2 = B.dma("s", scrS2[128 * ts:128 * ts + 128, :],
                              af_t[:, fo_s:fo_s + 1024], waits=(tsq2, tsm2))
                prev_s2sp = tk_s2
                seq_stats[ts] = (tk_s2, tsq2, tsm2)

        def p7_stream(sq=sq, seq_stats=seq_stats, sttdone=sttdone):
            tk_s2r = {}
            ty_prev = None
            ty_gates = {}
            prev_ln2 = None
            seq_toks = []
            for ts in range(16):
                yield from p7_ts(sq, ts, seq_stats, sttdone, tk_s2r, ty_gates,
                                 seq_toks, st := {'ty_prev': ty_prev,
                                                  'prev_ln2': prev_ln2})
                ty_prev = st['ty_prev']
                prev_ln2 = st['prev_ln2']
            p7state['gate'] = (last(seq_toks),)

        def p7_ts(sq, ts, seq_stats, sttdone, tk_s2r, ty_gates, seq_toks, st):
            ty_prev = st['ty_prev']
            prev_ln2 = st['prev_ln2']
            tk_s2, tsq2, tsm2 = seq_stats[ts]
            gate = tk_s2r.get(ts - 2)
            if ts in (0, 1):
                gate = sttdone[14 + ts]
            tk_s2r[ts] = B.dma("s",
                               af_t[:, fo_x0 + 1024 * (ts % 2):fo_x0 + 1024 * (ts % 2) + 1024],
                               scrS2[128 * ts:128 * ts + 128, :],
                               waits=(tk_s2, gate) if ts < 2 else
                               (tk_s2, ty_gates[ts - 2]))
            c1 = af_t[:, fo_c1 + 2:fo_c1 + 3]
            c2 = af_t[:, fo_c1 + 3:fo_c1 + 4]
            c3 = af_t[:, fo_c1 + 5:fo_c1 + 6]
            t1 = B.dve(lambda e, o=c1, i=af_t[:, fo_sm2 + ts:fo_sm2 + ts + 1]:
                       e.tensor_tensor(out=o, in0=i, in1=CB(1), op=ALU.mult),
                       waits=(tsm2, prev_ln2))
            t2 = B.dve(lambda e, o=c2, i=af_t[:, fo_sq2 + ts:fo_sq2 + ts + 1]:
                       e.tensor_tensor(out=o, in0=i, in1=CB(1), op=ALU.mult),
                       waits=(tsq2, prev_ln2))
            t3a = B.dve(lambda e, o=c3, b=c1:
                        e.tensor_tensor(out=o, in0=b, in1=b, op=ALU.mult), waits=(t1,))
            t3 = B.dve(lambda e, o=c2, a=c2, b=c3:
                       e.tensor_tensor(out=o, in0=a, in1=b, op=ALU.subtract),
                       waits=(t2, t3a))
            t4 = B.dve(lambda e, o=c2, a=c2:
                       e.tensor_tensor(out=o, in0=a, in1=CB(2), op=ALU.add), waits=(t3,))
            t5 = B.act(c2, c2, AF.Ln, waits=(t4,))
            t6 = B.act(af_t[:, fo_rstd2 + ts:fo_rstd2 + ts + 1], c2, AF.Exp, scale=-0.5,
                       waits=(t5,))
            t7 = B.dve(lambda e, o=af_t[:, fo_nmr2 + ts:fo_nmr2 + ts + 1], a=c1,
                       r=af_t[:, fo_rstd2 + ts:fo_rstd2 + ts + 1]:
                       e.scalar_tensor_tensor(out=o, in0=a, scalar=CB(3), in1=r,
                                              op0=ALU.mult, op1=ALU.mult), waits=(t6,))
            prev_ln2 = last([t7, t6])
            ty = B.act(af_t[:, fo_tmp:fo_tmp + 1024],
                       af_t[:, fo_x0 + 1024 * (ts % 2):fo_x0 + 1024 * (ts % 2) + 1024],
                       AF.Identity, scale=af_t[:, fo_rstd2 + ts:fo_rstd2 + ts + 1],
                       bias=af_t[:, fo_nmr2 + ts:fo_nmr2 + ts + 1],
                       waits=(t7, tk_s2r[ts], ty_prev))
            ty_gates[ts] = ty
            tk_y = B.dma("s", yout[sq, 128 * ts:128 * ts + 128, :],
                         af_t[:, fo_tmp:fo_tmp + 1024], waits=(ty,))
            st['ty_prev'] = tk_y
            st['prev_ln2'] = prev_ln2
            seq_toks.append(tk_y)
            y_toks.append(tk_y)
            yield

        if HOIST:
            bank_gate = last(list(last_obank.values()))
            gate_s4 = last(list(last_w2use.values()))
            w_gate = (last([gate_s4, seq_stats[15][1]]),)
            p3_gate = (gate_s4,)
            if sq + 1 < NSEQ:
                p2tok[sq + 1] = emit_p2(sq + 1, (last(list(last_gelu.values())),))
            p7gen = p7_stream()
            if sq + 1 == NSEQ:
                for _ in p7gen:
                    pass
                p7gen = None
        else:
            for _ in p7_stream():
                pass
            prev_seq_gate = p7state['gate']
            p3_gate = prev_seq_gate
            w_gate = prev_seq_gate
            p2gate = (last(list(last_gelu.values())),)

    B.emit("SP", lambda e: e.nop(), waits=(B.qtoks(),))

    sems = {}
    for s in CSEMS + QSEMS:
        sems[s] = E(nc.semaphore(s))
    block = E(nc.Block())
    B.realize(block, sems)
    ctx.close()
    return nc


def kernel(**inputs):
    x = _f32(inputs["L0_attn_query"])
    CS, MASK, AUX, CONSTB = _host_consts()
    ua, ub, da, db = _chains(_f32(inputs["L0_ffn_up_weights"]),
                             _f32(inputs["L0_ffn_down_weights"]))
    astk = np.zeros((D, 40), np.float32)
    astk[:, 0:8] = _f32(inputs["L0_attn_Aq"])
    astk[:, 32:40] = _f32(inputs["L0_attn_Av"])
    bstk = np.zeros((40, D), np.float32)
    bstk[0:8, :] = _f32(inputs["L0_attn_Bq"])
    bstk[32:40, :] = _f32(inputs["L0_attn_Bv"])

    common = {
        "wq": _f32(inputs["L0_attn_wq"]), "wk": _f32(inputs["L0_attn_wk"]),
        "wv": _f32(inputs["L0_attn_wv"]), "wo": _f32(inputs["L0_attn_wo"]),
        "astk": astk, "bstk": bstk,
        "wexp": _f32(inputs["L0_ffn_W_exp"]), "wcon": _f32(inputs["L0_ffn_W_con"]),
        "bexp": _f32(inputs["L0_ffn_b_exp"]), "upb": _f32(inputs["L0_ffn_up_bias"]),
        "bcon": _f32(inputs["L0_ffn_b_con"]), "dwnb": _f32(inputs["L0_ffn_down_bias"]),
        "cs_c": _bf(CS), "mask_c": _bf(MASK), "aux_c": _bf(AUX), "cb_c": CONSTB,
        "ch_ua": ua, "ch_ub": ub, "ch_da": da, "ch_db": db,
    }
    nc = build_program()
    in_maps = []
    for c in range(NCORES):
        m = dict(common)
        m["xin"] = np.ascontiguousarray(x[2 * c:2 * c + 2])
        in_maps.append(m)
    res = run_bass_kernel_spmd(nc, in_maps, core_ids=list(range(NCORES)))
    return np.concatenate([r["yout"] for r in res.results], axis=0)
